# version 30
# baseline (speedup 1.0000x reference)
import math

import ml_dtypes
import numpy as np

import concourse.bass as bass
import concourse.mybir as mybir
from concourse.bass_utils import run_bass_kernel_spmd

F32 = mybir.dt.float32
BF16 = mybir.dt.bfloat16
AF = mybir.ActivationFunctionType
ALU = mybir.AluOpType
AX = mybir.AxisListType

D = 2048
DC = 16
NH = 16
NKV = 4
HD = 128
INTER = 5632
IC = 44
TOK = 2048
NT = 16
HALO = 256
SINK = 4
BUF_T = HALO + TOK + SINK
NBT = 19
BT_PAD = 2320
NQI = 16
SPAN = 1024
NSPAN = 2
SC_N = 388
NEG = -1.0e30
QSCALE = 1.0 / math.sqrt(HD)
EPS = 1e-5

_DMA_SEMS_L = ["c", "cw", "xa", "xb", "xc", "ma", "mb", "x2a", "x2b", "qa", "qb",
               "wo0", "wo1", "wo2", "wo3", "ga", "gb", "ua", "ub",
               "ds", "sa", "sb"]
SEMS = tuple(["t", "v", "s"] + _DMA_SEMS_L)
DMA_SEMS = tuple(_DMA_SEMS_L)
ENG_SEM = {"tensor": "t", "vector": "v", "scalar": "s"}
ENGS = ("tensor", "vector", "scalar", "sync", "gpsimd")


def _make_mask(core):
    h = core % 2
    base = h * TOK
    qg = base + (128 * np.arange(NQI))[:, None, None] + \
        np.arange(128)[None, :, None]
    kb = 128 * np.arange(NQI)[:, None, None] + np.arange(384)[None, None, :]
    if h == 1:
        kg = np.where(kb < HALO, 1792 + kb, base + (kb - HALO))
        valid = np.ones(np.broadcast_shapes(qg.shape, kb.shape), dtype=bool)
    else:
        kg = np.where(kb < HALO, 0, base + (kb - HALO))
        valid = np.broadcast_to(kb >= HALO,
                                np.broadcast_shapes(qg.shape, kb.shape))
    ok = valid & (kg <= qg) & ((kg < SINK) | (kg >= qg - 255))
    m = np.full((NQI, 128, SC_N), NEG, dtype=np.float32)
    m[:, :, :384] = np.where(ok, 0.0, NEG)
    for qi in range(NQI):
        for j in range(SINK):
            if h == 0 and (128 * qi) <= (HALO + j) < (128 * qi + 384):
                continue
            qcol = base + 128 * qi + np.arange(128)
            m[qi, :, 384 + j] = np.where(j <= qcol, 0.0, NEG)
    return m


def _prep_lhsT(w, oc):
    return np.ascontiguousarray(w.reshape(oc, 128, DC, 128).transpose(0, 3, 2, 1))


def _prep_wvt(wv):
    return np.ascontiguousarray(
        np.ascontiguousarray(wv.T).reshape(DC, 128, 512).transpose(1, 0, 2))


def _prep_wot(wo):
    return np.ascontiguousarray(wo.reshape(4, 512, 16, 128).transpose(0, 3, 2, 1))


def _prep_dnt(dw):
    t = np.ascontiguousarray(dw.T).reshape(11, 4, 128, 4, 512)
    return np.ascontiguousarray(t.transpose(3, 0, 2, 1, 4))


class Prog:
    def __init__(self):
        self.q = {e: [] for e in ENGS}
        self.cnt = {s: 0 for s in SEMS}

    def op(self, eng, fn, waits, inc_sem, inc_n):
        w = {}
        own = ENG_SEM.get(eng)
        for tok in waits:
            if tok is None:
                continue
            s, v = tok
            if s in DMA_SEMS:
                v = self.cnt[s]
            if w.get(s, 0) < v:
                w[s] = v
        tok = None
        if inc_sem is not None:
            if inc_sem in DMA_SEMS and self.cnt[inc_sem] > 0:
                if w.get(inc_sem, 0) < self.cnt[inc_sem]:
                    w[inc_sem] = self.cnt[inc_sem]
            self.cnt[inc_sem] += inc_n
            tok = (inc_sem, self.cnt[inc_sem])
        self.q[eng].append((fn, sorted(w.items()), inc_sem, inc_n))
        return tok


class Res:
    __slots__ = ("w", "rs")

    def __init__(self):
        self.w = None
        self.rs = []


class Build:
    def __init__(self):
        self.P = Prog()

    def emit(self, eng, fn, reads=(), writes=(), alias=(), sem=None, n=None):
        waits = []
        for r in reads:
            waits.append(r.w)
        for r in writes:
            waits.append(r.w)
            waits.extend(r.rs)
        for r in alias:
            waits.append(r.w)
            waits.extend(r.rs)
        inc = sem if sem is not None else ENG_SEM[eng]
        inc_n = n if n is not None else (16 if inc in DMA_SEMS else 1)
        t = self.P.op(eng, fn, waits, inc, inc_n)
        for r in reads:
            r.rs.append(t)
        for r in writes:
            r.w = t
            r.rs = []
        return t

    def barrier_wait(self, eng, tokens):
        self.P.op(eng, None, [t for t in tokens if t], None, 0)


def build_nc(debug=False):
    DEBUG = [debug]
    nc = bass.Bass("TRN2", target_bir_lowering=False)
    B = Build()
    em = B.emit

    dxo = nc.dram_tensor("xo", [TOK, D], F32, kind="ExternalInput").ap()
    dxh = nc.dram_tensor("xh", [HALO, D], F32, kind="ExternalInput").ap()
    dxs = nc.dram_tensor("xs", [SINK, D], F32, kind="ExternalInput").ap()
    dmask = nc.dram_tensor("mask", [NQI, 128, SC_N], F32, kind="ExternalInput").ap()
    dwqt = nc.dram_tensor("wqt", [16, 128, DC, 128], F32, kind="ExternalInput").ap()
    dwkt = nc.dram_tensor("wkt", [4, 128, DC, 128], F32, kind="ExternalInput").ap()
    dwvt = nc.dram_tensor("wvt", [128, DC, 512], F32, kind="ExternalInput").ap()
    dwot = nc.dram_tensor("wot", [4, 128, 16, 512], F32, kind="ExternalInput").ap()
    dgt = nc.dram_tensor("gt", [IC, 128, DC, 128], F32, kind="ExternalInput").ap()
    dut = nc.dram_tensor("ut", [IC, 128, DC, 128], F32, kind="ExternalInput").ap()
    ddnt = nc.dram_tensor("dnt", [4, 11, 128, 4, 512], F32, kind="ExternalInput").ap()
    dgs = nc.dram_tensor("gsc", [128, IC], F32, kind="ExternalInput").ap()
    dus = nc.dram_tensor("usc", [128, IC], F32, kind="ExternalInput").ap()
    ddsb = nc.dram_tensor("dsb", [128, D], F32, kind="ExternalInput").ap()
    dide = nc.dram_tensor("ide", [128, 128], F32, kind="ExternalInput").ap()
    dcst = nc.dram_tensor("cst", [128, 4], F32, kind="ExternalInput").ap()
    dout = nc.dram_tensor("out", [TOK, D], F32, kind="ExternalOutput").ap()
    dx1 = nc.dram_tensor("x1spill", [TOK, D], BF16, kind="Internal").ap()

    SB_BASE = 16512

    def sb(name, off, shape, dtype):
        import math as _m
        nb = off + SB_BASE + int(np.prod(shape[1:])) * (
            4 if dtype == F32 else 2)
        assert nb <= 229376, (name, nb)
        return nc.alloc_sbuf_tensor_at(name, shape, dtype,
                                       offset=off + SB_BASE).ap()

    idf = sb("idf", 0, [128, 128], F32)
    stats = sb("stats", 512, [128, 24], F32)
    gs_sb = sb("gs_sb", 608, [128, IC], F32)
    us_sb = sb("us_sb", 800, [128, IC], F32)
    cst_sb = sb("cst_sb", 992, [128, 4], F32)

    xnT = sb("xnT", 1024, [128, DC, BT_PAD], BF16)
    aoall = sb("aoall", 1024, [128, NH, TOK], BF16)
    xbuf2 = [sb(f"xb2_{i}", 66560 + 4096 * i, [128, 1024], F32)
             for i in range(2)]
    h_a = sb("h_a", 1024, [128, 36, SPAN], BF16)

    kfm = sb("kfm", 75264, [128, NKV, BT_PAD], BF16)
    vtm = sb("vtm", 93824, [128, NBT, NKV, HD], BF16)
    x1_sb = sb("x1", 75264, [128, NT, D], BF16)

    qall = sb("qall", 113280, [128, NH, TOK], BF16)
    xbuf = [sb(f"xb_{i}", off, [128, D], F32)
            for i, off in enumerate((113280, 121472, 178816))]
    xnst = [sb(f"xnst_{i}", 129664 + 8192 * i, [128, D], F32)
            for i in range(2)]
    wkt_sb = sb("wkt_sb", 146048, [128, 4, DC, 128], BF16)
    wvt_sb = sb("wvt_sb", 162432, [128, DC, 512], BF16)

    wqb = [sb(f"wqb_{i}", 178816 + 4096 * i, [128, DC, 128], BF16)
           for i in range(2)]
    wot_sb = [
        sb("wot0", 178816, [128, 16, 512], BF16),
        sb("wot1", 140800, [128, 16, 512], BF16),
        sb("wot2", 157184, [128, 16, 512], BF16),
        sb("wot3", 195200, [128, 16, 512], BF16),
    ]

    exb = [sb(f"ex_{i}", 69696 + 800 * i, [128, 392], BF16)
           for i in range(4)]
    prb = [sb(f"pr_{i}", 198336 + 1568 * i, [128, 392], F32)
           for i in range(4)]
    ptb = [sb(f"pt_{i}", 204608 + 4096 * i, [128, 4, 4, 128], BF16)
           for i in range(2)]
    mkb = [sb(f"mk_{i}", 66560 + 1568 * i, [128, SC_N], F32)
           for i in range(2)]

    h_b = sb("h_b", 140800, [128, 8, SPAN], BF16)
    xn2T = sb("xn2T", 157184, [128, DC, SPAN], BF16)
    dsb_sb = sb("dsb_sb", 157184, [128, D], F32)
    gub = [sb(f"gub_{i}", 189952 + 4096 * i, [128, DC, 128], BF16)
           for i in range(4)]
    dnb = [sb(f"dnb_{i}", 189952 + 4096 * i, [128, 4, 512], BF16)
           for i in range(2)]
    dnf = [sb(f"dnf_{i}", off, [128, D], F32)
           for i, off in enumerate((165376, 177664))]
    gf = [[sb(f"gf_{s}{j}", 75264 + 32768 * s + 8192 * j, [128, D], F32)
           for j in range(2)] for s in range(2)]
    x1r = [[sb(f"x1r_{s}{j}", 75264 + 32768 * s + 4096 * j,
               [128, 4, 512], BF16) for j in range(2)]
           for s in range(2)]
    osb = [sb(f"osb_{i}", 198144 + 4096 * i, [128, 512], F32)
           for i in range(2)]
    xnst2 = sb("xnst2", 206336, [128, 1024], F32)
    lnjunk = sb("lnjunk", 173568, [128, 1024], F32)
    ajunk = sb("ajunk", 187008, [128, D], F32)
    silub = sb("silub", 210432, [128, 512], F32)

    pbt = [nc.alloc_psum_tensor(f"pb{i}", [128, 1024], F32).ap()
           for i in range(4)]

    def ph(i):
        return pbt[i // 2][:, 512 * (i % 2):512 * (i % 2) + 512]

    def dbg_dump(nm, ap_, rr):
        dd = nc.dram_tensor(nm, list(ap_.shape), ap_.dtype,
                            kind="ExternalOutput").ap()
        em("sync", lambda e, dd=dd, ap_=ap_: e.dma_start(out=dd, in_=ap_),
           reads=list(rr), sem="sa")

    r_pb = [Res() for _ in range(8)]
    r_xnT = [Res() for _ in range(NBT)]
    r_kfm, r_vtm, r_qall = Res(), Res(), Res()
    r_wkt, r_wvt = Res(), Res()
    r_wqb = [Res(), Res()]
    r_wot = [Res() for _ in range(4)]
    r_xbuf = [Res(), Res(), Res()]
    r_xnst = [Res(), Res()]
    r_xb2 = [Res(), Res()]
    r_ex = [Res() for _ in range(4)]
    r_pr = [Res() for _ in range(4)]
    r_pt = [Res(), Res()]
    r_mk = [Res(), Res()]
    r_ao = [Res() for _ in range(NT)]
    r_x1 = [Res() for _ in range(NT)]
    r_h = [Res() for _ in range(IC)]
    r_xn2T = [Res() for _ in range(8)]
    r_gub = [Res() for _ in range(4)]
    r_dnb = [Res(), Res()]
    r_dnf = [Res(), Res()]
    r_gf = [Res(), Res()]
    r_x1r = [Res(), Res()]
    r_x1sp = [Res() for _ in range(NT)]
    r_osb = [Res(), Res()]
    r_stats = [Res(), Res()]
    r_bst = [Res() for _ in range(4)]
    r_xnst2 = Res()
    r_lnj = Res()
    r_ajk = Res()
    r_red = [Res(), Res()]
    r_silu = Res()
    r_dsb = Res()
    r_misc = Res()

    nc.const_aps.register(F32, 1.0 / D, cst_sb[:, 0:1])
    nc.const_aps.register(F32, EPS, cst_sb[:, 1:2])
    nc.const_aps.register(F32, -1.0, cst_sb[:, 2:3])

    em("sync", lambda e: e.dma_start(out=cst_sb, in_=dcst), writes=[r_misc], sem="c")
    em("sync", lambda e: e.dma_start(out=idf, in_=dide), writes=[r_misc], sem="c")
    em("sync", lambda e: e.dma_start(out=gs_sb, in_=dgs), writes=[r_misc], sem="c")
    em("sync", lambda e: e.dma_start(out=us_sb, in_=dus), writes=[r_misc], sem="c")

    em("gpsimd", lambda e: e.dma_start(
        out=wkt_sb, in_=dwkt.rearrange("h p c i -> p h c i"),
        max_dma_last_dim=8192), writes=[r_wkt], sem="cw")
    em("gpsimd", lambda e: e.dma_start(
        out=wvt_sb, in_=dwvt, max_dma_last_dim=8192), writes=[r_wvt], sem="cw")

    def ln_front(src_ap, rows, read_res, sset, alias1=()):
        st = stats[0:rows, 8 * sset:8 * sset + 8]
        s_nm, s_sq = st[:, 0:1], st[:, 1:2]
        rs = [read_res] if read_res is not None else []
        em("vector", lambda e: e.reduce_sum(s_nm, src_ap, AX.X, negate=True),
           reads=rs, writes=[r_red[sset]])
        em("scalar", lambda e: e.activation(ajunk[0:rows, :], src_ap,
                                            AF.Square, accum_out=s_sq),
           reads=rs, writes=[r_stats[sset], r_ajk], alias=alias1)

    def ln_back(src_ap, rows, xout, xout_res, read_res, sset):
        st = stats[0:rows, 8 * sset:8 * sset + 8]
        s_nm, s_sq, s_mn, s_m2, s_ve, s_var, s_sd, s_rs = (
            st[:, i:i + 1] for i in range(8))
        s_nb = s_var
        rs = [read_res] if read_res is not None else []
        em("vector", lambda e: e.tensor_scalar_mul(s_mn, s_nm, 1.0 / D),
           reads=[r_red[sset]], writes=[r_stats[sset]])
        em("vector", lambda e: e.tensor_scalar(s_ve, s_sq, 1.0 / D, EPS,
                                               ALU.mult, ALU.add),
           writes=[r_stats[sset]])
        em("vector", lambda e: e.tensor_mul(s_m2, s_mn, s_mn),
           writes=[r_stats[sset]])
        em("vector", lambda e: e.tensor_sub(s_var, s_ve, s_m2),
           writes=[r_stats[sset]])
        em("scalar", lambda e: e.activation(s_sd, s_var, AF.Sqrt),
           writes=[r_stats[sset]])
        em("vector", lambda e: e.reciprocal(s_rs, s_sd),
           writes=[r_stats[sset]])
        em("vector", lambda e: e.tensor_mul(s_nb, s_mn, s_rs),
           writes=[r_stats[sset]])
        em("vector", lambda e: e.tensor_scalar(xout, src_ap, s_rs, s_nb,
                                               ALU.mult, ALU.add),
           reads=rs + [r_stats[sset]], writes=[xout_res])

    def transpose_tile(xn_ap, xn_res, rows, dst, dst_res, dst_col,
                       extra_alias=()):
        for hf in range(4):
            bank = 4 + hf
            em("tensor", lambda e, hf=hf, rw=rows: [
                e.transpose(out=ph(4 + hf)[:, 128 * c:128 * c + rw],
                            in_=xn_ap[:, 128 * (4 * hf + c):
                                      128 * (4 * hf + c) + 128],
                            identity=idf[0:rw, 0:rw])
                for c in range(4)],
               reads=[xn_res, r_misc], writes=[r_pb[bank]], n=4)
            evac_eng = "scalar" if hf % 2 == 0 else "vector"

            def _evac(e, hf=hf, rw=rows):
                view = ph(4 + hf).rearrange("p (c q) -> p c q", c=4)[:, :, 0:rw]
                dstv = dst[:, 4 * hf:4 * hf + 4, dst_col:dst_col + rw]
                return (e.copy(dstv, view) if hf % 2 == 0
                        else e.tensor_copy(dstv, view))
            em(evac_eng, _evac,
               reads=[r_pb[bank]], writes=[dst_res], alias=extra_alias)

    def a_src(b):
        if b < 2:
            return dxh[128 * b:128 * b + 128]
        if b < 18:
            return dxo[128 * (b - 2):128 * (b - 2) + 128]
        return dxs

    def a_tr_stage(b):
        rows = 4 if b == NBT - 1 else 128
        sset = b % 2
        transpose_tile(xnst[sset][0:rows, :], r_xnst[sset], rows,
                       xnT, r_xnT[b], 128 * b)

    def a_proj_stage(b):
        rows = 4 if b == NBT - 1 else 128
        vb = b % 2
        em("tensor", lambda e, b=b, rw=rows, vb=vb: [
            e.matmul(ph(vb)[0:rw, :],
                     lhsT=xnT[:, dc, 128 * b:128 * b + rw],
                     rhs=wvt_sb[:, dc, :],
                     start=(dc == 0), stop=(dc == DC - 1))
            for dc in range(DC)][-1],
           reads=[r_xnT[b], r_wvt], writes=[r_pb[vb]])
        em("scalar", lambda e, b=b, rw=rows, vb=vb: e.copy(
            vtm[0:rw, b, :, :],
            ph(vb)[0:rw, :].rearrange("p (k h) -> p k h", k=NKV)),
           reads=[r_pb[vb]], writes=[r_vtm])
        if b % 4 == 3 or b == NBT - 1:
            g = b // 4
            g0 = 512 * g
            ng = min(512, BUF_T - g0)
            tiles = [r_xnT[tt] for tt in range(4 * g, min(4 * g + 4, NBT))]
            for hc in range(4):
                kb = 2 + (hc % 2)
                em("tensor", lambda e, hc=hc, g0=g0, ng=ng, kb=kb: [
                    e.matmul(ph(kb)[:, 0:ng],
                             lhsT=wkt_sb[:, hc, dc, :],
                             rhs=xnT[:, dc, g0:g0 + ng],
                             start=(dc == 0), stop=(dc == DC - 1))
                    for dc in range(DC)][-1],
                   reads=tiles + [r_wkt], writes=[r_pb[kb]])
                em("scalar", lambda e, hc=hc, g0=g0, ng=ng, kb=kb: e.copy(
                    kfm[:, hc, g0:g0 + ng], ph(kb)[:, 0:ng]),
                   reads=[r_pb[kb]], writes=[r_kfm])

    for b in range(NBT + 3):
        if b < NBT:
            rows = 4 if b == NBT - 1 else 128
            sl = b % 3
            em("sync", lambda e, s=a_src(b), sl=sl, rw=rows: e.dma_start(
                out=xbuf[sl][0:rw, :], in_=s), writes=[r_xbuf[sl]],
               sem=("xa", "xb", "xc")[sl])
            ln_front(xbuf[sl][0:rows, :], rows, r_xbuf[sl], b % 2)
        if 1 <= b <= NBT:
            bb = b - 1
            rows = 4 if bb == NBT - 1 else 128
            sl = bb % 3
            sset = bb % 2
            ln_back(xbuf[sl][0:rows, :], rows, xnst[sset][0:rows, :],
                    r_xnst[sset], r_xbuf[sl], sset)
        if 2 <= b <= NBT + 1:
            a_tr_stage(b - 2)
        if 3 <= b:
            a_proj_stage(b - 3)

    for hc in range(16):
        sl = hc % 2
        em("gpsimd", lambda e, hc=hc, sl=sl: e.dma_start(
            out=wqb[sl], in_=dwqt[hc], max_dma_last_dim=8192),
           writes=[r_wqb[sl]],
           alias=([r_xbuf[2]] if hc < 2 else ()),
           sem=("qa", "qb")[sl])
        for g in range(4):
            qb = 4 + ((hc * 4 + g) % 2)
            tiles = [r_xnT[2 + 4 * g + i] for i in range(4)]
            em("tensor", lambda e, hc=hc, g=g, sl=sl, qb=qb: [
                e.matmul(ph(qb),
                         lhsT=wqb[sl][:, dc, :],
                         rhs=xnT[:, dc, HALO + 512 * g:HALO + 512 * g + 512],
                         start=(dc == 0), stop=(dc == DC - 1))
                for dc in range(DC)][-1],
               reads=tiles + [r_wqb[sl]], writes=[r_pb[qb]])
            em("vector", lambda e, hc=hc, g=g, qb=qb: e.tensor_scalar_mul(
                qall[:, hc, 512 * g:512 * g + 512], ph(qb), QSCALE),
               reads=[r_pb[qb]], writes=[r_qall],
               alias=([r_xbuf[0], r_xbuf[1], r_xnst[0], r_xnst[1],
                       r_wkt, r_wvt] if (hc == 0 and g == 0) else ()))

    if debug:
        dbg_dump("d_xnT", xnT, r_xnT)

    em("gpsimd", lambda e: e.dma_start(out=wot_sb[0], in_=dwot[0],
                                       max_dma_last_dim=8192),
       writes=[r_wot[0]], alias=[r_wqb[0], r_wqb[1], r_ajk], sem="wo0")

    st_neg = [stats[:, 8 + i:9 + i] for i in range(4)]
    st_sum = [stats[:, 12 + i:13 + i] for i in range(4)]
    st_den = [stats[:, 16 + i:17 + i] for i in range(4)]
    st_rec = [stats[:, 20 + i:21 + i] for i in range(4)]

    def sm_scores(qi, kv, j):
        head = 4 * kv + j
        sbk = j
        em("tensor", lambda e, qi=qi, head=head, sbk=sbk: [
            e.matmul(ph(sbk)[:, 0:384],
                     lhsT=qall[:, head, 128 * qi:128 * qi + 128],
                     rhs=kfm[:, head // 4, 128 * qi:128 * qi + 384],
                     start=True, stop=True),
            e.matmul(ph(sbk)[:, 384:SC_N],
                     lhsT=qall[:, head, 128 * qi:128 * qi + 128],
                     rhs=kfm[:, head // 4, 2304:2308],
                     start=True, stop=True)],
           reads=[r_qall, r_kfm], writes=[r_pb[sbk]], n=2)

    def sm_stats(qi, kv, j):
        sbk = j
        al0 = ([r_stats[0], r_stats[1], r_red[0], r_red[1]]
               if (qi == 0 and kv == 0 and j == 0) else ())
        em("vector", lambda e, qi=qi, sbk=sbk: e.tensor_add(
            ph(sbk)[:, 0:SC_N], ph(sbk)[:, 0:SC_N], mkb[qi % 2]),
           reads=[r_mk[qi % 2]], writes=[r_pb[sbk]])
        em("vector", lambda e, sbk=sbk, j=j: e.reduce_max(
            st_neg[j], ph(sbk)[:, 0:SC_N], AX.X, negate=True),
           reads=[r_pb[sbk]], writes=[r_bst[j]], alias=al0)

    def sm_exp(qi, kv, j):
        sbk = j
        em("scalar", lambda e, sbk=sbk, j=j: e.activation(
            exb[j][:, 0:SC_N], ph(sbk)[:, 0:SC_N], AF.Exp,
            bias=st_neg[j], accum_out=st_sum[j]),
           reads=[r_pb[sbk], r_bst[j]], writes=[r_ex[j], r_bst[j]],
           alias=([r_xnT[tt] for tt in range(NBT)]
                  if (qi == 0 and kv == 0) else ()))

    def sm_norm(qi, kv, j):
        em("vector", lambda e, j=j: e.tensor_scalar_max(
            st_den[j], st_sum[j], 1e-9), writes=[r_bst[j]])
        em("vector", lambda e, j=j: e.reciprocal(
            st_rec[j], st_den[j]), writes=[r_bst[j]])
        em("vector", lambda e, j=j: e.tensor_scalar_mul(
            prb[j][:, 0:SC_N], exb[j][:, 0:SC_N], st_rec[j]),
           reads=[r_ex[j], r_bst[j]], writes=[r_pr[j]])

    def transpose_head(qi, kv, j):
        it = qi * 4 + kv
        psl = it % 2
        pslot = j
        tb = 6 + (j % 2)
        em("tensor", lambda e, pslot=pslot, tb=tb: [
            e.transpose(out=ph(tb)[:, 128 * s:128 * s + 128],
                        in_=prb[pslot][:, 128 * s:128 * s + 128],
                        identity=idf)
            for s in range(3)] + [
            e.transpose(out=ph(tb)[0:4, 384:512],
                        in_=prb[pslot][:, 384:SC_N],
                        identity=idf)],
           reads=[r_pr[pslot], r_misc], writes=[r_pb[tb]], n=4)
        em("scalar", lambda e, j=j, tb=tb, psl=psl: [
            e.copy(ptb[psl][:, 0:3, j, :],
                   ph(tb)[:, 0:384].rearrange("p (s q) -> p s q", s=3)),
            e.copy(ptb[psl][0:4, 3, j, :],
                   ph(tb)[0:4, 384:512])],
           reads=[r_pb[tb]], writes=[r_pt[psl]], n=2)

    def pv_evac(qi, kv):
        it = qi * 4 + kv
        psl = it % 2
        ab = 4 + (it % 2)
        em("tensor", lambda e, qi=qi, kv=kv, psl=psl, ab=ab: [
            e.matmul(ph(ab), lhsT=vtm[:, qi + s, kv, :],
                     rhs=ptb[psl][:, s, :, :], start=(s == 0), stop=False)
            for s in range(3)] + [
            e.matmul(ph(ab), lhsT=vtm[0:4, 18, kv, :],
                     rhs=ptb[psl][0:4, 3, :, :], start=False, stop=True)],
           reads=[r_pt[psl], r_vtm], writes=[r_pb[ab]], n=4)
        if DEBUG[0] and qi == 4 and kv == 0:
            dbg_dump("d_pt", ptb[psl], [r_pt[psl]])
        em("scalar", lambda e, qi=qi, kv=kv, ab=ab: e.copy(
            aoall[:, 4 * kv:4 * kv + 4, 128 * qi:128 * qi + 128],
            ph(ab).rearrange("p (j q) -> p j q", j=4)),
           reads=[r_pb[ab]], writes=[r_ao[qi]],
           alias=([r_xnT[t] for t in range(NBT)]
                  if (qi == 0 and kv == 0) else ()))

    prev = None
    for qi in range(NQI):
        em("sync", lambda e, qi=qi: e.dma_start(out=mkb[qi % 2],
                                                in_=dmask[qi]),
           writes=[r_mk[qi % 2]],
           alias=([r_xnT[tt] for tt in range(NBT)] if qi < 2 else ()),
           sem=("ma", "mb")[qi % 2])
        for kv in range(NKV):
            for j in range(4):
                if prev is not None:
                    transpose_head(prev[0], prev[1], j)
                sm_scores(qi, kv, j)
            for j in range(4):
                sm_stats(qi, kv, j)
            for j in range(4):
                sm_exp(qi, kv, j)
            for j in range(4):
                sm_norm(qi, kv, j)
            if prev is not None:
                pv_evac(*prev)
            prev = (qi, kv)
    for j in range(4):
        transpose_head(prev[0], prev[1], j)
    pv_evac(*prev)

    for g in range(1, 4):
        al = [r_qall] if g in (1, 2) else \
            list(r_ex) + list(r_pr) + [r_pt[0], r_pt[1]]
        em("gpsimd", lambda e, g=g: e.dma_start(
            out=wot_sb[g], in_=dwot[g], max_dma_last_dim=8192),
           writes=[r_wot[g]], alias=al, sem=f"wo{g}")

    if debug:
        dbg_dump("d_kfm", kfm, [r_kfm])
        dbg_dump("d_vtm", vtm, [r_vtm])
        dbg_dump("d_qall", qall, [r_qall])
        dbg_dump("d_ao", aoall, [r_ao[t] for t in range(NT)])

    def h_ap(i, tt):
        buf, ii = (h_a, i) if i < 36 else (h_b, i - 36)
        return buf[:, ii, 128 * tt:128 * tt + 128]

    def ln2_stats(sp, tt):
        t = 8 * sp + tt
        w = stats[:, 16:24]
        w_nm, w_sq, w_sq2, w_sqt, w_mn, w_m2, w_sd, w_ve = (
            w[:, i:i + 1] for i in range(8))
        w_var = w_sq
        f_rs = stats[:, 2 * tt:2 * tt + 1]
        f_nb = stats[:, 2 * tt + 1:2 * tt + 2]
        al1 = (list(r_bst) if (sp == 0 and tt == 0) else ())
        alj = ([r_qall] if (sp == 0 and tt == 0) else
               ([r_xn2T[k] for k in range(8)] if (sp == 1 and tt == 0)
                else ()))
        em("vector", lambda e: e.reduce_sum(w_nm, x1_sb[:, t, :], AX.X,
                                            negate=True),
           reads=[r_x1[t]], writes=[r_bst[0]], alias=al1)
        em("scalar", lambda e: e.activation(
            lnjunk, x1_sb[:, t, 0:1024], AF.Square, accum_out=w_sq),
           reads=[r_x1[t]], writes=[r_bst[0], r_lnj], alias=alj)
        em("scalar", lambda e: e.activation(
            lnjunk, x1_sb[:, t, 1024:2048], AF.Square, accum_out=w_sq2),
           reads=[r_x1[t]], writes=[r_bst[0], r_lnj])
        em("scalar", lambda e: e.activation(w_mn, w_nm, AF.Identity,
                                            scale=1.0 / D),
           reads=[r_misc], writes=[r_bst[0]])
        em("scalar", lambda e: e.activation(w_sqt, w_sq, AF.Identity,
                                            bias=w_sq2),
           writes=[r_bst[0]])
        em("scalar", lambda e: e.activation(w_m2, w_mn, AF.Square),
           writes=[r_bst[0]])
        em("scalar", lambda e: e.activation(w_ve, w_sqt, AF.Identity,
                                            scale=1.0 / D, bias=EPS),
           writes=[r_bst[0]])
        em("scalar", lambda e: e.activation(w_var, w_m2, AF.Identity,
                                            scale=-1.0, bias=w_ve),
           writes=[r_bst[0]])
        em("scalar", lambda e: e.activation(w_sd, w_var, AF.Sqrt),
           writes=[r_bst[0]])
        em("vector", lambda e: e.reciprocal(f_rs, w_sd),
           reads=[r_bst[0]], writes=[r_stats[0], r_stats[1]])
        em("scalar", lambda e: e.activation(f_nb, w_mn, AF.Identity,
                                            scale=f_rs),
           reads=[r_bst[0]], writes=[r_stats[0], r_stats[1]])

    def ln2_apply(sp, tt):
        t = 8 * sp + tt
        f_rs = stats[:, 2 * tt:2 * tt + 1]
        f_nb = stats[:, 2 * tt + 1:2 * tt + 2]
        for hh in range(2):
            em("scalar", lambda e, hh=hh: e.activation(
                xnst2, x1_sb[:, t, 1024 * hh:1024 * hh + 1024],
                AF.Identity, bias=f_nb, scale=f_rs),
               reads=[r_x1[t], r_stats[0], r_stats[1]], writes=[r_xnst2],
               alias=([r_pt[0], r_pt[1], r_wot[3]]
                      if (sp == 0 and tt == 0 and hh == 0) else ()))
            for sub in range(2):
                bank = 4 + 2 * hh + sub
                em("tensor", lambda e, sub=sub, bank=bank: [
                    e.transpose(out=ph(bank)[:, 128 * c:128 * c + 128],
                                in_=xnst2[:, 128 * (4 * sub + c):
                                          128 * (4 * sub + c) + 128],
                                identity=idf)
                    for c in range(4)],
                   reads=[r_xnst2, r_misc], writes=[r_pb[bank]], n=4)
                dc0 = 8 * hh + 4 * sub
                em("vector", lambda e, dc0=dc0, bank=bank, tt=tt:
                   e.tensor_copy(
                       xn2T[:, dc0:dc0 + 4, 128 * tt:128 * tt + 128],
                       ph(bank).rearrange("p (c q) -> p c q", c=4)),
                   reads=[r_pb[bank]], writes=[r_xn2T[tt]],
                   alias=([r_wot[2], r_wot[0], r_qall, r_dsb, r_lnj,
                           r_dnf[0], r_dnf[1]]
                          if (tt == 0 and hh == 0 and sub == 0) else ()))

    for t in range(NT):
        for half in range(2):
            em("sync", lambda e, t=t, half=half: e.dma_start(
                out=xbuf2[half],
                in_=dxo[128 * t:128 * t + 128,
                        1024 * half:1024 * half + 1024]),
               writes=[r_xb2[half]],
               alias=(([r_xnT[tt] for tt in range(NBT)] +
                       [r_mk[0], r_mk[1]] + list(r_ex)) if t == 0 else ()),
               sem=("x2a", "x2b")[half])
        for dmg in range(4):
            ob = dmg if t % 2 == 0 else 4 + dmg
            em("tensor", lambda e, t=t, dmg=dmg, ob=ob: [
                e.matmul(ph(ob), lhsT=aoall[:, hh, 128 * t:128 * t + 128],
                         rhs=wot_sb[dmg][:, hh, :],
                         start=(hh == 0), stop=(hh == NH - 1))
                for hh in range(NH)][-1],
               reads=[r_ao[t], r_wot[dmg]], writes=[r_pb[ob]])
            em("vector", lambda e, t=t, dmg=dmg, ob=ob: e.tensor_add(
                x1_sb[:, t, 512 * dmg:512 * dmg + 512], ph(ob),
                xbuf2[dmg // 2][:, 512 * (dmg % 2):512 * (dmg % 2) + 512]),
               reads=[r_pb[ob], r_xb2[dmg // 2]], writes=[r_x1[t]],
               alias=([r_kfm, r_vtm, r_qall]
                      if (t == 0 and dmg == 0) else ()))
        if t % 8 < 4:
            em("sync", lambda e, t=t: e.dma_start(
                out=dx1[128 * t:128 * t + 128, :], in_=x1_sb[:, t, :]),
               reads=[r_x1[t]], writes=[r_x1sp[t]], sem="sa")
        if t < 8:
            ln2_stats(0, t)

    for sp in range(NSPAN):
        for tt in range(8):
            ln2_apply(sp, tt)
        def g_dma(i):
            gsl = i % 2
            em("sync", lambda e, i=i, gsl=gsl, sp=sp: e.dma_start(
                out=gf[sp][gsl],
                in_=dgt[i].rearrange("p c h -> p (c h)")),
               writes=[r_gf[gsl]],
               alias=([r_x1[8 * sp + 2 * gsl], r_x1[8 * sp + 2 * gsl + 1]]
                      if i < 2 else ()),
               sem=("ga", "gb")[gsl])

        def g_cast(i):
            gsl = i % 2
            al = ([r_wot[0], r_wot[3], r_pt[0], r_pt[1]] +
                  list(r_ex) + list(r_pr) + [r_dnb[gsl]]
                  if (sp == 0 and i < 2) else
                  ([r_dnb[gsl]] if i < 2 else []))
            em("scalar", lambda e, gsl=gsl, sp=sp: e.copy(
                gub[gsl].rearrange("p c h -> p (c h)"), gf[sp][gsl]),
               reads=[r_gf[gsl]], writes=[r_gub[gsl]], alias=al)

        g_dma(0)
        g_cast(0)
        g_dma(1)
        for i in range(IC):
            gsl = i % 2
            usl = 2 + (i % 2)
            if i + 2 < IC:
                g_dma(i + 2)
            if i + 1 < IC:
                g_cast(i + 1)
            al_g = ([r_wot[0], r_wot[3], r_pt[0], r_pt[1]] +
                    list(r_ex) + list(r_pr)
                    if (sp == 0 and i < 2) else ())
            em("gpsimd", lambda e, i=i, usl=usl: e.dma_start(
                out=gub[usl], in_=dut[i], max_dma_last_dim=8192),
               writes=[r_gub[usl]],
               alias=list(al_g) + [r_osb[usl - 2]], sem=("ua", "ub")[usl - 2])
            for grp in range(2):
                gb = 4 * (i % 2) + grp
                ub = 4 * (i % 2) + 2 + grp
                xres = [r_xn2T[4 * grp + k] for k in range(4)]
                em("tensor", lambda e, grp=grp, gsl=gsl, gb=gb: [
                    e.matmul(ph(gb), lhsT=gub[gsl][:, dc, :],
                             rhs=xn2T[:, dc, 512 * grp:512 * grp + 512],
                             start=(dc == 0), stop=(dc == DC - 1))
                    for dc in range(DC)][-1],
                   reads=xres + [r_gub[gsl]], writes=[r_pb[gb]])
                em("tensor", lambda e, grp=grp, usl=usl, ub=ub: [
                    e.matmul(ph(ub), lhsT=gub[usl][:, dc, :],
                             rhs=xn2T[:, dc, 512 * grp:512 * grp + 512],
                             start=(dc == 0), stop=(dc == DC - 1))
                    for dc in range(DC)][-1],
                   reads=xres + [r_gub[usl]], writes=[r_pb[ub]])
                em("scalar", lambda e, i=i, gb=gb: e.activation(
                    silub, ph(gb), AF.Silu, scale=gs_sb[:, i:i + 1]),
                   reads=[r_pb[gb], r_misc], writes=[r_silu],
                   alias=([r_pt[0], r_pt[1], r_wot[3]]
                          if (sp == 0 and i == 0 and grp == 0) else ()))
                if sp == 0 and i == 0 and grp == 0:
                    hal = ([r_ao[t] for t in range(NT)] +
                           [r_xb2[0], r_xb2[1], r_mk[0], r_mk[1]] +
                           list(r_ex))
                elif sp == 0 and i == 36 and grp == 0:
                    hal = [r_wot[1]]
                else:
                    hal = ()
                em("vector", lambda e, i=i, grp=grp, ub=ub:
                   e.scalar_tensor_tensor(
                       (h_a if i < 36 else h_b)[:, (i if i < 36 else i - 36),
                                                512 * grp:512 * grp + 512],
                       ph(ub), us_sb[:, i:i + 1], silub,
                       ALU.mult, ALU.mult),
                   reads=[r_pb[ub], r_silu, r_misc], writes=[r_h[i]],
                   alias=hal)
        em("sync", lambda e: e.dma_start(out=dsb_sb, in_=ddsb),
           writes=[r_dsb], alias=[r_xn2T[k] for k in range(8)], sem="ds")
        NCH = 4 * 11

        def dn_dma(ci):
            dmg_, i4_ = divmod(ci, 11)
            dsl = ci % 2
            em("sync", lambda e, dmg_=dmg_, i4_=i4_, dsl=dsl: e.dma_start(
                out=dnf[dsl],
                in_=ddnt[dmg_, i4_].rearrange("p k h -> p (k h)")),
               writes=[r_dnf[dsl]],
               alias=([r_xn2T[kk] for kk in range(8)] if ci == 0 else ()),
               sem="ds")

        def dn_cast(ci):
            dsl = ci % 2
            em("scalar", lambda e, dsl=dsl: e.copy(
                dnb[dsl].rearrange("p k h -> p (k h)"), dnf[dsl]),
               reads=[r_dnf[dsl]], writes=[r_dnb[dsl]],
               alias=([r_gub[dsl]] if ci < 2 else ()))

        def x1_reload(d):
            xsl = d % 2
            for tt in range(4):
                em("sync", lambda e, d=d, xsl=xsl, tt=tt, sp=sp: e.dma_start(
                    out=x1r[sp][xsl][:, tt, :],
                    in_=dx1[128 * (8 * sp + tt):128 * (8 * sp + tt) + 128,
                            512 * d:512 * d + 512]),
                   reads=[r_x1sp[8 * sp + tt]], writes=[r_x1r[xsl]],
                   alias=([r_gf[0], r_gf[1]] if tt == 0 else ()),
                   sem=("x2a", "x2b")[xsl])

        x1_reload(0)
        dn_dma(0)
        dn_cast(0)
        dn_dma(1)
        for ci in range(NCH):
            dmg, i4 = divmod(ci, 11)
            dsl = ci % 2
            if ci + 2 < NCH:
                dn_dma(ci + 2)
            if ci + 1 < NCH:
                dn_cast(ci + 1)
            if i4 == 5 and dmg + 1 < 4:
                x1_reload(dmg + 1)
            if sp == 0 and 4 <= ci < 36 and ci % 4 == 0:
                ln2_stats(1, (ci - 4) // 4)
            for k in range(4):
                i = 4 * i4 + k
                if i == 0:
                    for tt in range(8):
                        em("tensor", lambda e, tt=tt, dsl=dsl:
                           e.matmul(ph(tt), lhsT=h_ap(0, tt),
                                    rhs=dnb[dsl][:, 0, :],
                                    start=True, stop=False),
                           reads=[r_h[0], r_dnb[dsl]],
                           writes=[r_pb[tt]])
                    continue
                em("tensor", lambda e, i=i, k=k, dsl=dsl: [
                    e.matmul(ph(tt), lhsT=h_ap(i, tt),
                             rhs=dnb[dsl][:, k, :],
                             start=(i == 0), stop=(i == IC - 1))
                    for tt in range(8)],
                   reads=[r_h[i], r_dnb[dsl]],
                   writes=[r_pb[tt] for tt in range(8)], n=8)
            if i4 != 10:
                continue
            for tt in range(8):
                t = 8 * sp + tt
                oslot = tt % 2
                em("vector", lambda e, t=t, tt=tt, dmg=dmg, oslot=oslot:
                   e.tensor_mul(osb[oslot], ph(tt),
                                dsb_sb[:, 512 * dmg:512 * dmg + 512]),
                   reads=[r_pb[tt], r_dsb], writes=[r_osb[oslot]],
                   alias=([r_gub[2 + oslot]]
                          if (sp == 0 and dmg == 0 and tt < 2) else ()))
                if tt < 4:
                    xsrc = x1r[sp][dmg % 2][:, tt, :]
                    xread = r_x1r[dmg % 2]
                else:
                    xsrc = x1_sb[:, t, 512 * dmg:512 * dmg + 512]
                    xread = r_x1[t]
                em("vector", lambda e, oslot=oslot, xsrc=xsrc:
                   e.tensor_add(osb[oslot], osb[oslot], xsrc),
                   reads=[xread], writes=[r_osb[oslot]])
                em("sync", lambda e, t=t, dmg=dmg, oslot=oslot: e.dma_start(
                    out=dout[128 * t:128 * t + 128,
                             512 * dmg:512 * dmg + 512],
                    in_=osb[oslot]),
                   reads=[r_osb[oslot]], sem=("sa", "sb")[oslot])

    if debug:
        dbg_dump("d_x1", x1_sb, [r_x1[t] for t in range(NT)])

    B.barrier_wait("sync", [("sa", B.P.cnt["sa"]), ("sb", B.P.cnt["sb"])])

    import contextlib
    stack = contextlib.ExitStack()
    sem_handles = {s: stack.enter_context(nc.semaphore(s)) for s in SEMS}
    blk = stack.enter_context(nc.Block())

    def runner(queue):
        def run(eng):
            for fn, waits, inc_sem, inc_n in queue:
                for s, v in waits:
                    eng.wait_ge(sem_handles[s], v)
                if fn is None:
                    continue
                inst = fn(eng)
                if isinstance(inst, list):
                    assert len(inst) == inc_n, (len(inst), inc_n)
                    for i_ in inst:
                        i_.then_inc(sem_handles[inc_sem], 1)
                elif inc_sem is not None:
                    inst.then_inc(sem_handles[inc_sem], inc_n)
        return run

    blk.tensor(runner(B.P.q["tensor"]))
    blk.vector(runner(B.P.q["vector"]))
    blk.scalar(runner(B.P.q["scalar"]))
    blk.sync(runner(B.P.q["sync"]))
    blk.gpsimd(runner(B.P.q["gpsimd"]))
    stack.close()
    return nc


_NC_CACHE = None


def kernel(L0_attn_x, L0_attn_wq, L0_attn_wk, L0_attn_wv, L0_attn_wo,
           L0_ffn_gate_w, L0_ffn_up_w, L0_ffn_down_w, L0_ffn_gate_s,
           L0_ffn_up_s, L0_ffn_down_s):
    global _NC_CACHE
    x = np.asarray(L0_attn_x, dtype=np.float32)
    wq = np.asarray(L0_attn_wq, np.float32)
    wk = np.asarray(L0_attn_wk, np.float32)
    wv = np.asarray(L0_attn_wv, np.float32)
    wo = np.asarray(L0_attn_wo, np.float32)
    gw = np.asarray(L0_ffn_gate_w, np.float32)
    uw = np.asarray(L0_ffn_up_w, np.float32)
    dw = np.asarray(L0_ffn_down_w, np.float32)
    gs = np.asarray(L0_ffn_gate_s, np.float32)
    us = np.asarray(L0_ffn_up_s, np.float32)
    ds = np.asarray(L0_ffn_down_s, np.float32)

    shared = {
        "wqt": _prep_lhsT(wq, 16),
        "wkt": _prep_lhsT(wk, 4),
        "wvt": _prep_wvt(wv),
        "wot": _prep_wot(wo),
        "gt": _prep_lhsT(gw, IC),
        "ut": _prep_lhsT(uw, IC),
        "dnt": _prep_dnt(dw),
        "gsc": np.ascontiguousarray(gs.reshape(IC, 128).T),
        "usc": np.ascontiguousarray(us.reshape(IC, 128).T),
        "dsb": np.ascontiguousarray(np.broadcast_to(ds, (128, D))),
        "ide": np.eye(128, dtype=np.float32),
        "cst": np.ascontiguousarray(np.broadcast_to(
            np.array([1.0 / D, EPS, -1.0, 0.0], np.float32), (128, 4))),
    }
    in_maps = []
    for core in range(8):
        s, h = core // 2, core % 2
        m = dict(shared)
        m["xo"] = np.ascontiguousarray(x[s, h * TOK:(h + 1) * TOK])
        m["xh"] = np.ascontiguousarray(
            x[s, 1792:2048] if h == 1 else x[s, 0:HALO])
        m["xs"] = np.ascontiguousarray(x[s, 0:SINK])
        m["mask"] = _make_mask(core)
        in_maps.append(m)

    if _NC_CACHE is None:
        _NC_CACHE = build_nc()
    res = run_bass_kernel_spmd(_NC_CACHE, in_maps, core_ids=list(range(8)))
    out = np.empty((4, 4096, D), dtype=np.float32)
    for core in range(8):
        s, h = core // 2, core % 2
        out[s, h * TOK:(h + 1) * TOK] = res.results[core]["out"]
    return out


# version 32
# speedup vs baseline: 1.0116x; 1.0116x over previous
import math

import ml_dtypes
import numpy as np

import concourse.bass as bass
import concourse.mybir as mybir
from concourse.bass_utils import run_bass_kernel_spmd

F32 = mybir.dt.float32
BF16 = mybir.dt.bfloat16
AF = mybir.ActivationFunctionType
ALU = mybir.AluOpType
AX = mybir.AxisListType

D = 2048
DC = 16
NH = 16
NKV = 4
HD = 128
INTER = 5632
IC = 44
TOK = 2048
NT = 16
HALO = 256
SINK = 4
BUF_T = HALO + TOK + SINK
NBT = 19
BT_PAD = 2320
NQI = 16
SPAN = 1024
NSPAN = 2
SC_N = 388
NEG = -1.0e30
QSCALE = 1.0 / math.sqrt(HD)
EPS = 1e-5

_DMA_SEMS_L = ["c", "cw", "xa", "xb", "xc", "ma", "mb", "x2a", "x2b", "qa", "qb",
               "wo0", "wo1", "wo2", "wo3", "ga", "gb", "ua", "ub",
               "ds", "sa", "sb"]
SEMS = tuple(["t", "v", "s"] + _DMA_SEMS_L)
DMA_SEMS = tuple(_DMA_SEMS_L)
ENG_SEM = {"tensor": "t", "vector": "v", "scalar": "s"}
ENGS = ("tensor", "vector", "scalar", "sync", "gpsimd")


def _make_mask(core):
    h = core % 2
    base = h * TOK
    qg = base + (128 * np.arange(NQI))[:, None, None] + \
        np.arange(128)[None, :, None]
    kb = 128 * np.arange(NQI)[:, None, None] + np.arange(384)[None, None, :]
    if h == 1:
        kg = np.where(kb < HALO, 1792 + kb, base + (kb - HALO))
        valid = np.ones(np.broadcast_shapes(qg.shape, kb.shape), dtype=bool)
    else:
        kg = np.where(kb < HALO, 0, base + (kb - HALO))
        valid = np.broadcast_to(kb >= HALO,
                                np.broadcast_shapes(qg.shape, kb.shape))
    ok = valid & (kg <= qg) & ((kg < SINK) | (kg >= qg - 255))
    m = np.full((NQI, 128, SC_N), NEG, dtype=np.float32)
    m[:, :, :384] = np.where(ok, 0.0, NEG)
    for qi in range(NQI):
        for j in range(SINK):
            if h == 0 and (128 * qi) <= (HALO + j) < (128 * qi + 384):
                continue
            qcol = base + 128 * qi + np.arange(128)
            m[qi, :, 384 + j] = np.where(j <= qcol, 0.0, NEG)
    return m


def _prep_lhsT(w, oc):
    return np.ascontiguousarray(w.reshape(oc, 128, DC, 128).transpose(0, 3, 2, 1))


def _prep_wvt(wv):
    return np.ascontiguousarray(
        np.ascontiguousarray(wv.T).reshape(DC, 128, 512).transpose(1, 0, 2))


def _prep_wot(wo):
    return np.ascontiguousarray(wo.reshape(4, 512, 16, 128).transpose(0, 3, 2, 1))


def _prep_dnt(dw):
    t = np.ascontiguousarray(dw.T).reshape(11, 4, 128, 4, 512)
    return np.ascontiguousarray(t.transpose(3, 0, 2, 1, 4))


class Prog:
    def __init__(self):
        self.q = {e: [] for e in ENGS}
        self.cnt = {s: 0 for s in SEMS}

    def op(self, eng, fn, waits, inc_sem, inc_n):
        w = {}
        own = ENG_SEM.get(eng)
        for tok in waits:
            if tok is None:
                continue
            s, v = tok
            if s in DMA_SEMS:
                v = self.cnt[s]
            if w.get(s, 0) < v:
                w[s] = v
        tok = None
        if inc_sem is not None:
            if inc_sem in DMA_SEMS and self.cnt[inc_sem] > 0:
                if w.get(inc_sem, 0) < self.cnt[inc_sem]:
                    w[inc_sem] = self.cnt[inc_sem]
            self.cnt[inc_sem] += inc_n
            tok = (inc_sem, self.cnt[inc_sem])
        self.q[eng].append((fn, sorted(w.items()), inc_sem, inc_n))
        return tok


class Res:
    __slots__ = ("w", "rs")

    def __init__(self):
        self.w = None
        self.rs = []


class Build:
    def __init__(self):
        self.P = Prog()

    def emit(self, eng, fn, reads=(), writes=(), alias=(), sem=None, n=None):
        waits = []
        for r in reads:
            waits.append(r.w)
        for r in writes:
            waits.append(r.w)
            waits.extend(r.rs)
        for r in alias:
            waits.append(r.w)
            waits.extend(r.rs)
        inc = sem if sem is not None else ENG_SEM[eng]
        inc_n = n if n is not None else (16 if inc in DMA_SEMS else 1)
        t = self.P.op(eng, fn, waits, inc, inc_n)
        for r in reads:
            r.rs.append(t)
        for r in writes:
            r.w = t
            r.rs = []
        return t

    def barrier_wait(self, eng, tokens):
        self.P.op(eng, None, [t for t in tokens if t], None, 0)


def build_nc(debug=False):
    DEBUG = [debug]
    nc = bass.Bass("TRN2", target_bir_lowering=False)
    B = Build()
    em = B.emit

    dxo = nc.dram_tensor("xo", [TOK, D], F32, kind="ExternalInput").ap()
    dxh = nc.dram_tensor("xh", [HALO, D], F32, kind="ExternalInput").ap()
    dxs = nc.dram_tensor("xs", [SINK, D], F32, kind="ExternalInput").ap()
    dmask = nc.dram_tensor("mask", [NQI, 128, SC_N], F32, kind="ExternalInput").ap()
    dwqt = nc.dram_tensor("wqt", [16, 128, DC, 128], F32, kind="ExternalInput").ap()
    dwkt = nc.dram_tensor("wkt", [4, 128, DC, 128], F32, kind="ExternalInput").ap()
    dwvt = nc.dram_tensor("wvt", [128, DC, 512], F32, kind="ExternalInput").ap()
    dwot = nc.dram_tensor("wot", [4, 128, 16, 512], F32, kind="ExternalInput").ap()
    dgt = nc.dram_tensor("gt", [IC, 128, DC, 128], F32, kind="ExternalInput").ap()
    dut = nc.dram_tensor("ut", [IC, 128, DC, 128], F32, kind="ExternalInput").ap()
    ddnt = nc.dram_tensor("dnt", [4, 11, 128, 4, 512], F32, kind="ExternalInput").ap()
    dgs = nc.dram_tensor("gsc", [128, IC], F32, kind="ExternalInput").ap()
    dus = nc.dram_tensor("usc", [128, IC], F32, kind="ExternalInput").ap()
    ddsb = nc.dram_tensor("dsb", [128, D], F32, kind="ExternalInput").ap()
    dide = nc.dram_tensor("ide", [128, 128], F32, kind="ExternalInput").ap()
    dcst = nc.dram_tensor("cst", [128, 4], F32, kind="ExternalInput").ap()
    dout = nc.dram_tensor("out", [TOK, D], F32, kind="ExternalOutput").ap()
    dx1 = nc.dram_tensor("x1spill", [TOK, D], BF16, kind="Internal").ap()

    SB_BASE = 16512

    def sb(name, off, shape, dtype):
        import math as _m
        nb = off + SB_BASE + int(np.prod(shape[1:])) * (
            4 if dtype == F32 else 2)
        assert nb <= 229376, (name, nb)
        return nc.alloc_sbuf_tensor_at(name, shape, dtype,
                                       offset=off + SB_BASE).ap()

    idf = sb("idf", 0, [128, 128], F32)
    stats = sb("stats", 512, [128, 24], F32)
    gs_sb = sb("gs_sb", 608, [128, IC], F32)
    us_sb = sb("us_sb", 800, [128, IC], F32)
    cst_sb = sb("cst_sb", 992, [128, 4], F32)

    xnT = sb("xnT", 1024, [128, DC, BT_PAD], BF16)
    aoall = sb("aoall", 1024, [128, NH, TOK], BF16)
    xbuf2 = [sb(f"xb2_{i}", 66560 + 4096 * i, [128, 1024], F32)
             for i in range(2)]
    h_a = sb("h_a", 1024, [128, 36, SPAN], BF16)

    kfm = sb("kfm", 75264, [128, NKV, BT_PAD], BF16)
    vtm = sb("vtm", 93824, [128, NBT, NKV, HD], BF16)
    x1_sb = sb("x1", 75264, [128, NT, D], BF16)

    qall = sb("qall", 113280, [128, NH, TOK], BF16)
    xbuf = [sb(f"xb_{i}", off, [128, D], F32)
            for i, off in enumerate((113280, 121472, 178816))]
    xnst = [sb(f"xnst_{i}", 129664 + 8192 * i, [128, D], F32)
            for i in range(2)]
    wkt_sb = sb("wkt_sb", 146048, [128, 4, DC, 128], BF16)
    wvt_sb = sb("wvt_sb", 162432, [128, DC, 512], BF16)

    wqb = [sb(f"wqb_{i}", 178816 + 4096 * i, [128, DC, 128], BF16)
           for i in range(2)]
    wot_sb = [
        sb("wot0", 178816, [128, 16, 512], BF16),
        sb("wot1", 140800, [128, 16, 512], BF16),
        sb("wot2", 157184, [128, 16, 512], BF16),
        sb("wot3", 195200, [128, 16, 512], BF16),
    ]

    exb = [sb(f"ex_{i}", 69696 + 800 * i, [128, 392], BF16)
           for i in range(4)]
    prb = [sb(f"pr_{i}", 198336 + 1568 * i, [128, 392], F32)
           for i in range(4)]
    ptb = [sb(f"pt_{i}", 204608 + 4096 * i, [128, 4, 4, 128], BF16)
           for i in range(2)]
    mkb = [sb(f"mk_{i}", 66560 + 1568 * i, [128, SC_N], F32)
           for i in range(2)]

    h_b = sb("h_b", 140800, [128, 8, SPAN], BF16)
    xn2T = sb("xn2T", 157184, [128, DC, SPAN], BF16)
    dsb_sb = sb("dsb_sb", 157184, [128, D], F32)
    gub = [sb(f"gub_{i}", 189952 + 4096 * i, [128, DC, 128], BF16)
           for i in range(4)]
    dnb = [sb(f"dnb_{i}", 189952 + 4096 * i, [128, 4, 512], BF16)
           for i in range(2)]
    dnb.append(sb("dnb_2", 206336, [128, 4, 512], BF16))
    dnf = [sb(f"dnf_{i}", off, [128, D], F32)
           for i, off in enumerate((165376, 177664))]
    gf = [[sb(f"gf_{s}{j}", 75264 + 32768 * s + 8192 * j, [128, D], F32)
           for j in range(2)] for s in range(2)]
    x1r = [[sb(f"x1r_{s}{j}", 75264 + 32768 * s + 4096 * j,
               [128, 4, 512], BF16) for j in range(2)]
           for s in range(2)]
    osb = [sb(f"osb_{i}", 198144 + 4096 * i, [128, 512], F32)
           for i in range(2)]
    xnst2 = sb("xnst2", 206336, [128, 1024], F32)
    lnjunk = sb("lnjunk", 173568, [128, 1024], F32)
    ajunk = sb("ajunk", 187008, [128, D], F32)
    silub = sb("silub", 210432, [128, 512], F32)

    pbt = [nc.alloc_psum_tensor(f"pb{i}", [128, 1024], F32).ap()
           for i in range(4)]

    def ph(i):
        return pbt[i // 2][:, 512 * (i % 2):512 * (i % 2) + 512]

    def dbg_dump(nm, ap_, rr):
        dd = nc.dram_tensor(nm, list(ap_.shape), ap_.dtype,
                            kind="ExternalOutput").ap()
        em("sync", lambda e, dd=dd, ap_=ap_: e.dma_start(out=dd, in_=ap_),
           reads=list(rr), sem="sa")

    r_pb = [Res() for _ in range(8)]
    r_xnT = [Res() for _ in range(NBT)]
    r_kfm, r_vtm, r_qall = Res(), Res(), Res()
    r_wkt, r_wvt = Res(), Res()
    r_wqb = [Res(), Res()]
    r_wot = [Res() for _ in range(4)]
    r_xbuf = [Res(), Res(), Res()]
    r_xnst = [Res(), Res()]
    r_xb2 = [Res(), Res()]
    r_ex = [Res() for _ in range(4)]
    r_pr = [Res() for _ in range(4)]
    r_pt = [Res(), Res()]
    r_mk = [Res(), Res()]
    r_ao = [Res() for _ in range(NT)]
    r_x1 = [Res() for _ in range(NT)]
    r_h = [Res() for _ in range(IC)]
    r_xn2T = [Res() for _ in range(8)]
    r_gub = [Res() for _ in range(4)]
    r_dnb = [Res(), Res(), Res()]
    r_dnf = [Res(), Res()]
    r_gf = [Res(), Res()]
    r_x1r = [Res(), Res()]
    r_x1sp = [Res() for _ in range(NT)]
    r_osb = [Res(), Res()]
    r_stats = [Res(), Res()]
    r_bst = [Res() for _ in range(4)]
    r_xnst2 = Res()
    r_lnj = Res()
    r_ajk = Res()
    r_red = [Res(), Res()]
    r_silu = Res()
    r_dsb = Res()
    r_misc = Res()

    nc.const_aps.register(F32, 1.0 / D, cst_sb[:, 0:1])
    nc.const_aps.register(F32, EPS, cst_sb[:, 1:2])
    nc.const_aps.register(F32, -1.0, cst_sb[:, 2:3])

    em("sync", lambda e: e.dma_start(out=cst_sb, in_=dcst), writes=[r_misc], sem="c")
    em("sync", lambda e: e.dma_start(out=idf, in_=dide), writes=[r_misc], sem="c")
    em("sync", lambda e: e.dma_start(out=gs_sb, in_=dgs), writes=[r_misc], sem="c")
    em("sync", lambda e: e.dma_start(out=us_sb, in_=dus), writes=[r_misc], sem="c")

    em("gpsimd", lambda e: e.dma_start(
        out=wkt_sb, in_=dwkt.rearrange("h p c i -> p h c i"),
        max_dma_last_dim=8192), writes=[r_wkt], sem="cw")
    em("gpsimd", lambda e: e.dma_start(
        out=wvt_sb, in_=dwvt, max_dma_last_dim=8192), writes=[r_wvt], sem="cw")

    def ln_front(src_ap, rows, read_res, sset, alias1=()):
        st = stats[0:rows, 8 * sset:8 * sset + 8]
        s_nm, s_sq = st[:, 0:1], st[:, 1:2]
        rs = [read_res] if read_res is not None else []
        em("vector", lambda e: e.reduce_sum(s_nm, src_ap, AX.X, negate=True),
           reads=rs, writes=[r_red[sset]])
        em("scalar", lambda e: e.activation(ajunk[0:rows, :], src_ap,
                                            AF.Square, accum_out=s_sq),
           reads=rs, writes=[r_stats[sset], r_ajk], alias=alias1)

    def ln_back(src_ap, rows, xout, xout_res, read_res, sset):
        st = stats[0:rows, 8 * sset:8 * sset + 8]
        s_nm, s_sq, s_mn, s_m2, s_ve, s_var, s_sd, s_rs = (
            st[:, i:i + 1] for i in range(8))
        s_nb = s_var
        rs = [read_res] if read_res is not None else []
        em("vector", lambda e: e.tensor_scalar_mul(s_mn, s_nm, 1.0 / D),
           reads=[r_red[sset]], writes=[r_stats[sset]])
        em("vector", lambda e: e.tensor_scalar(s_ve, s_sq, 1.0 / D, EPS,
                                               ALU.mult, ALU.add),
           writes=[r_stats[sset]])
        em("vector", lambda e: e.tensor_mul(s_m2, s_mn, s_mn),
           writes=[r_stats[sset]])
        em("vector", lambda e: e.tensor_sub(s_var, s_ve, s_m2),
           writes=[r_stats[sset]])
        em("scalar", lambda e: e.activation(s_sd, s_var, AF.Sqrt),
           writes=[r_stats[sset]])
        em("vector", lambda e: e.reciprocal(s_rs, s_sd),
           writes=[r_stats[sset]])
        em("vector", lambda e: e.tensor_mul(s_nb, s_mn, s_rs),
           writes=[r_stats[sset]])
        em("vector", lambda e: e.tensor_scalar(xout, src_ap, s_rs, s_nb,
                                               ALU.mult, ALU.add),
           reads=rs + [r_stats[sset]], writes=[xout_res])

    def transpose_tile(xn_ap, xn_res, rows, dst, dst_res, dst_col,
                       extra_alias=()):
        for hf in range(4):
            bank = 4 + hf
            em("tensor", lambda e, hf=hf, rw=rows: [
                e.transpose(out=ph(4 + hf)[:, 128 * c:128 * c + rw],
                            in_=xn_ap[:, 128 * (4 * hf + c):
                                      128 * (4 * hf + c) + 128],
                            identity=idf[0:rw, 0:rw])
                for c in range(4)],
               reads=[xn_res, r_misc], writes=[r_pb[bank]], n=4)
            evac_eng = "scalar" if hf % 2 == 0 else "vector"

            def _evac(e, hf=hf, rw=rows):
                view = ph(4 + hf).rearrange("p (c q) -> p c q", c=4)[:, :, 0:rw]
                dstv = dst[:, 4 * hf:4 * hf + 4, dst_col:dst_col + rw]
                return (e.copy(dstv, view) if hf % 2 == 0
                        else e.tensor_copy(dstv, view))
            em(evac_eng, _evac,
               reads=[r_pb[bank]], writes=[dst_res], alias=extra_alias)

    def a_src(b):
        if b < 2:
            return dxh[128 * b:128 * b + 128]
        if b < 18:
            return dxo[128 * (b - 2):128 * (b - 2) + 128]
        return dxs

    def a_tr_stage(b):
        rows = 4 if b == NBT - 1 else 128
        sset = b % 2
        transpose_tile(xnst[sset][0:rows, :], r_xnst[sset], rows,
                       xnT, r_xnT[b], 128 * b)

    def a_proj_stage(b):
        rows = 4 if b == NBT - 1 else 128
        vb = b % 2
        em("tensor", lambda e, b=b, rw=rows, vb=vb: [
            e.matmul(ph(vb)[0:rw, :],
                     lhsT=xnT[:, dc, 128 * b:128 * b + rw],
                     rhs=wvt_sb[:, dc, :],
                     start=(dc == 0), stop=(dc == DC - 1))
            for dc in range(DC)][-1],
           reads=[r_xnT[b], r_wvt], writes=[r_pb[vb]])
        em("scalar", lambda e, b=b, rw=rows, vb=vb: e.copy(
            vtm[0:rw, b, :, :],
            ph(vb)[0:rw, :].rearrange("p (k h) -> p k h", k=NKV)),
           reads=[r_pb[vb]], writes=[r_vtm])
        if b % 4 == 3 or b == NBT - 1:
            g = b // 4
            g0 = 512 * g
            ng = min(512, BUF_T - g0)
            tiles = [r_xnT[tt] for tt in range(4 * g, min(4 * g + 4, NBT))]
            for hc in range(4):
                kb = 2 + (hc % 2)
                em("tensor", lambda e, hc=hc, g0=g0, ng=ng, kb=kb: [
                    e.matmul(ph(kb)[:, 0:ng],
                             lhsT=wkt_sb[:, hc, dc, :],
                             rhs=xnT[:, dc, g0:g0 + ng],
                             start=(dc == 0), stop=(dc == DC - 1))
                    for dc in range(DC)][-1],
                   reads=tiles + [r_wkt], writes=[r_pb[kb]])
                em("scalar", lambda e, hc=hc, g0=g0, ng=ng, kb=kb: e.copy(
                    kfm[:, hc, g0:g0 + ng], ph(kb)[:, 0:ng]),
                   reads=[r_pb[kb]], writes=[r_kfm])

    for b in range(NBT + 3):
        if b < NBT:
            rows = 4 if b == NBT - 1 else 128
            sl = b % 3
            em("sync", lambda e, s=a_src(b), sl=sl, rw=rows: e.dma_start(
                out=xbuf[sl][0:rw, :], in_=s), writes=[r_xbuf[sl]],
               sem=("xa", "xb", "xc")[sl])
            ln_front(xbuf[sl][0:rows, :], rows, r_xbuf[sl], b % 2)
        if 1 <= b <= NBT:
            bb = b - 1
            rows = 4 if bb == NBT - 1 else 128
            sl = bb % 3
            sset = bb % 2
            ln_back(xbuf[sl][0:rows, :], rows, xnst[sset][0:rows, :],
                    r_xnst[sset], r_xbuf[sl], sset)
        if 2 <= b <= NBT + 1:
            a_tr_stage(b - 2)
        if 3 <= b:
            a_proj_stage(b - 3)

    for hc in range(16):
        sl = hc % 2
        em("gpsimd", lambda e, hc=hc, sl=sl: e.dma_start(
            out=wqb[sl], in_=dwqt[hc], max_dma_last_dim=8192),
           writes=[r_wqb[sl]],
           alias=([r_xbuf[2]] if hc < 2 else ()),
           sem=("qa", "qb")[sl])
        for g in range(4):
            qb = 4 + ((hc * 4 + g) % 2)
            tiles = [r_xnT[2 + 4 * g + i] for i in range(4)]
            em("tensor", lambda e, hc=hc, g=g, sl=sl, qb=qb: [
                e.matmul(ph(qb),
                         lhsT=wqb[sl][:, dc, :],
                         rhs=xnT[:, dc, HALO + 512 * g:HALO + 512 * g + 512],
                         start=(dc == 0), stop=(dc == DC - 1))
                for dc in range(DC)][-1],
               reads=tiles + [r_wqb[sl]], writes=[r_pb[qb]])
            em("vector", lambda e, hc=hc, g=g, qb=qb: e.tensor_scalar_mul(
                qall[:, hc, 512 * g:512 * g + 512], ph(qb), QSCALE),
               reads=[r_pb[qb]], writes=[r_qall],
               alias=([r_xbuf[0], r_xbuf[1], r_xnst[0], r_xnst[1],
                       r_wkt, r_wvt] if (hc == 0 and g == 0) else ()))

    if debug:
        dbg_dump("d_xnT", xnT, r_xnT)

    em("gpsimd", lambda e: e.dma_start(out=wot_sb[0], in_=dwot[0],
                                       max_dma_last_dim=8192),
       writes=[r_wot[0]], alias=[r_wqb[0], r_wqb[1], r_ajk], sem="wo0")

    st_neg = [stats[:, 8 + i:9 + i] for i in range(4)]
    st_sum = [stats[:, 12 + i:13 + i] for i in range(4)]
    st_den = [stats[:, 16 + i:17 + i] for i in range(4)]
    st_rec = [stats[:, 20 + i:21 + i] for i in range(4)]

    def sm_scores(qi, kv, j):
        head = 4 * kv + j
        sbk = j
        em("tensor", lambda e, qi=qi, head=head, sbk=sbk: [
            e.matmul(ph(sbk)[:, 0:384],
                     lhsT=qall[:, head, 128 * qi:128 * qi + 128],
                     rhs=kfm[:, head // 4, 128 * qi:128 * qi + 384],
                     start=True, stop=True),
            e.matmul(ph(sbk)[:, 384:SC_N],
                     lhsT=qall[:, head, 128 * qi:128 * qi + 128],
                     rhs=kfm[:, head // 4, 2304:2308],
                     start=True, stop=True)],
           reads=[r_qall, r_kfm], writes=[r_pb[sbk]], n=2)

    def sm_stats(qi, kv, j):
        sbk = j
        al0 = ([r_stats[0], r_stats[1], r_red[0], r_red[1]]
               if (qi == 0 and kv == 0 and j == 0) else ())
        em("vector", lambda e, qi=qi, sbk=sbk: e.tensor_add(
            ph(sbk)[:, 0:SC_N], ph(sbk)[:, 0:SC_N], mkb[qi % 2]),
           reads=[r_mk[qi % 2]], writes=[r_pb[sbk]])
        em("vector", lambda e, sbk=sbk, j=j: e.reduce_max(
            st_neg[j], ph(sbk)[:, 0:SC_N], AX.X, negate=True),
           reads=[r_pb[sbk]], writes=[r_bst[j]], alias=al0)

    def sm_exp(qi, kv, j):
        sbk = j
        em("scalar", lambda e, sbk=sbk, j=j: e.activation(
            exb[j][:, 0:SC_N], ph(sbk)[:, 0:SC_N], AF.Exp,
            bias=st_neg[j], accum_out=st_sum[j]),
           reads=[r_pb[sbk], r_bst[j]], writes=[r_ex[j], r_bst[j]],
           alias=([r_xnT[tt] for tt in range(NBT)]
                  if (qi == 0 and kv == 0) else ()))

    def sm_norm(qi, kv, j):
        em("vector", lambda e, j=j: e.tensor_scalar_max(
            st_den[j], st_sum[j], 1e-9), writes=[r_bst[j]])
        em("vector", lambda e, j=j: e.reciprocal(
            st_rec[j], st_den[j]), writes=[r_bst[j]])
        em("vector", lambda e, j=j: e.tensor_scalar_mul(
            prb[j][:, 0:SC_N], exb[j][:, 0:SC_N], st_rec[j]),
           reads=[r_ex[j], r_bst[j]], writes=[r_pr[j]])

    def transpose_head(qi, kv, j):
        it = qi * 4 + kv
        psl = it % 2
        pslot = j
        tb = 6 + (j % 2)
        em("tensor", lambda e, pslot=pslot, tb=tb: [
            e.transpose(out=ph(tb)[:, 128 * s:128 * s + 128],
                        in_=prb[pslot][:, 128 * s:128 * s + 128],
                        identity=idf)
            for s in range(3)] + [
            e.transpose(out=ph(tb)[0:4, 384:512],
                        in_=prb[pslot][:, 384:SC_N],
                        identity=idf)],
           reads=[r_pr[pslot], r_misc], writes=[r_pb[tb]], n=4)
        em("scalar", lambda e, j=j, tb=tb, psl=psl: [
            e.copy(ptb[psl][:, 0:3, j, :],
                   ph(tb)[:, 0:384].rearrange("p (s q) -> p s q", s=3)),
            e.copy(ptb[psl][0:4, 3, j, :],
                   ph(tb)[0:4, 384:512])],
           reads=[r_pb[tb]], writes=[r_pt[psl]], n=2)

    def pv_evac(qi, kv):
        it = qi * 4 + kv
        psl = it % 2
        ab = 4 + (it % 2)
        em("tensor", lambda e, qi=qi, kv=kv, psl=psl, ab=ab: [
            e.matmul(ph(ab), lhsT=vtm[:, qi + s, kv, :],
                     rhs=ptb[psl][:, s, :, :], start=(s == 0), stop=False)
            for s in range(3)] + [
            e.matmul(ph(ab), lhsT=vtm[0:4, 18, kv, :],
                     rhs=ptb[psl][0:4, 3, :, :], start=False, stop=True)],
           reads=[r_pt[psl], r_vtm], writes=[r_pb[ab]], n=4)
        if DEBUG[0] and qi == 4 and kv == 0:
            dbg_dump("d_pt", ptb[psl], [r_pt[psl]])
        em("scalar", lambda e, qi=qi, kv=kv, ab=ab: e.copy(
            aoall[:, 4 * kv:4 * kv + 4, 128 * qi:128 * qi + 128],
            ph(ab).rearrange("p (j q) -> p j q", j=4)),
           reads=[r_pb[ab]], writes=[r_ao[qi]],
           alias=([r_xnT[t] for t in range(NBT)]
                  if (qi == 0 and kv == 0) else ()))

    prev = None
    for qi in range(NQI):
        em("sync", lambda e, qi=qi: e.dma_start(out=mkb[qi % 2],
                                                in_=dmask[qi]),
           writes=[r_mk[qi % 2]],
           alias=([r_xnT[tt] for tt in range(NBT)] if qi < 2 else ()),
           sem=("ma", "mb")[qi % 2])
        for kv in range(NKV):
            for j in range(4):
                if prev is not None:
                    transpose_head(prev[0], prev[1], j)
                sm_scores(qi, kv, j)
            for j in range(4):
                sm_stats(qi, kv, j)
            for j in range(4):
                sm_exp(qi, kv, j)
            for j in range(4):
                sm_norm(qi, kv, j)
            if prev is not None:
                pv_evac(*prev)
            prev = (qi, kv)
    for j in range(4):
        transpose_head(prev[0], prev[1], j)
    pv_evac(*prev)

    for g in range(1, 4):
        al = [r_qall] if g in (1, 2) else \
            list(r_ex) + list(r_pr) + [r_pt[0], r_pt[1]]
        em("gpsimd", lambda e, g=g: e.dma_start(
            out=wot_sb[g], in_=dwot[g], max_dma_last_dim=8192),
           writes=[r_wot[g]], alias=al, sem=f"wo{g}")

    if debug:
        dbg_dump("d_kfm", kfm, [r_kfm])
        dbg_dump("d_vtm", vtm, [r_vtm])
        dbg_dump("d_qall", qall, [r_qall])
        dbg_dump("d_ao", aoall, [r_ao[t] for t in range(NT)])

    def h_ap(i, tt):
        buf, ii = (h_a, i) if i < 36 else (h_b, i - 36)
        return buf[:, ii, 128 * tt:128 * tt + 128]

    def ln2_stats(sp, tt):
        t = 8 * sp + tt
        w = stats[:, 16:24]
        w_nm, w_sq, w_sq2, w_sqt, w_mn, w_m2, w_sd, w_ve = (
            w[:, i:i + 1] for i in range(8))
        w_var = w_sq
        f_rs = stats[:, 2 * tt:2 * tt + 1]
        f_nb = stats[:, 2 * tt + 1:2 * tt + 2]
        al1 = (list(r_bst) if (sp == 0 and tt == 0) else ())
        alj = ([r_qall] if (sp == 0 and tt == 0) else
               ([r_xn2T[k] for k in range(8)] if (sp == 1 and tt == 0)
                else ()))
        em("vector", lambda e: e.reduce_sum(w_nm, x1_sb[:, t, :], AX.X,
                                            negate=True),
           reads=[r_x1[t]], writes=[r_bst[0]], alias=al1)
        em("scalar", lambda e: e.activation(
            lnjunk, x1_sb[:, t, 0:1024], AF.Square, accum_out=w_sq),
           reads=[r_x1[t]], writes=[r_bst[0], r_lnj], alias=alj)
        em("scalar", lambda e: e.activation(
            lnjunk, x1_sb[:, t, 1024:2048], AF.Square, accum_out=w_sq2),
           reads=[r_x1[t]], writes=[r_bst[0], r_lnj])
        em("scalar", lambda e: e.activation(w_mn, w_nm, AF.Identity,
                                            scale=1.0 / D),
           reads=[r_misc], writes=[r_bst[0]])
        em("scalar", lambda e: e.activation(w_sqt, w_sq, AF.Identity,
                                            bias=w_sq2),
           writes=[r_bst[0]])
        em("scalar", lambda e: e.activation(w_m2, w_mn, AF.Square),
           writes=[r_bst[0]])
        em("scalar", lambda e: e.activation(w_ve, w_sqt, AF.Identity,
                                            scale=1.0 / D, bias=EPS),
           writes=[r_bst[0]])
        em("scalar", lambda e: e.activation(w_var, w_m2, AF.Identity,
                                            scale=-1.0, bias=w_ve),
           writes=[r_bst[0]])
        em("scalar", lambda e: e.activation(w_sd, w_var, AF.Sqrt),
           writes=[r_bst[0]])
        em("vector", lambda e: e.reciprocal(f_rs, w_sd),
           reads=[r_bst[0]], writes=[r_stats[0], r_stats[1]])
        em("scalar", lambda e: e.activation(f_nb, w_mn, AF.Identity,
                                            scale=f_rs),
           reads=[r_bst[0]], writes=[r_stats[0], r_stats[1]])

    def ln2_apply(sp, tt):
        t = 8 * sp + tt
        f_rs = stats[:, 2 * tt:2 * tt + 1]
        f_nb = stats[:, 2 * tt + 1:2 * tt + 2]
        for hh in range(2):
            em("scalar", lambda e, hh=hh: e.activation(
                xnst2, x1_sb[:, t, 1024 * hh:1024 * hh + 1024],
                AF.Identity, bias=f_nb, scale=f_rs),
               reads=[r_x1[t], r_stats[0], r_stats[1]], writes=[r_xnst2],
               alias=([r_pt[0], r_pt[1], r_wot[3]]
                      if (sp == 0 and tt == 0 and hh == 0) else
                      ([r_dnb[2]] if (tt == 0 and hh == 0) else ())))
            for sub in range(2):
                bank = 4 + 2 * hh + sub
                em("tensor", lambda e, sub=sub, bank=bank: [
                    e.transpose(out=ph(bank)[:, 128 * c:128 * c + 128],
                                in_=xnst2[:, 128 * (4 * sub + c):
                                          128 * (4 * sub + c) + 128],
                                identity=idf)
                    for c in range(4)],
                   reads=[r_xnst2, r_misc], writes=[r_pb[bank]], n=4)
                dc0 = 8 * hh + 4 * sub
                em("vector", lambda e, dc0=dc0, bank=bank, tt=tt:
                   e.tensor_copy(
                       xn2T[:, dc0:dc0 + 4, 128 * tt:128 * tt + 128],
                       ph(bank).rearrange("p (c q) -> p c q", c=4)),
                   reads=[r_pb[bank]], writes=[r_xn2T[tt]],
                   alias=([r_wot[2], r_wot[0], r_qall, r_dsb, r_lnj,
                           r_dnf[0], r_dnf[1]]
                          if (tt == 0 and hh == 0 and sub == 0) else ()))

    for t in range(NT):
        for half in range(2):
            em("sync", lambda e, t=t, half=half: e.dma_start(
                out=xbuf2[half],
                in_=dxo[128 * t:128 * t + 128,
                        1024 * half:1024 * half + 1024]),
               writes=[r_xb2[half]],
               alias=(([r_xnT[tt] for tt in range(NBT)] +
                       [r_mk[0], r_mk[1]] + list(r_ex)) if t == 0 else ()),
               sem=("x2a", "x2b")[half])
        for dmg in range(4):
            ob = dmg if t % 2 == 0 else 4 + dmg
            em("tensor", lambda e, t=t, dmg=dmg, ob=ob: [
                e.matmul(ph(ob), lhsT=aoall[:, hh, 128 * t:128 * t + 128],
                         rhs=wot_sb[dmg][:, hh, :],
                         start=(hh == 0), stop=(hh == NH - 1))
                for hh in range(NH)][-1],
               reads=[r_ao[t], r_wot[dmg]], writes=[r_pb[ob]])
            em("vector", lambda e, t=t, dmg=dmg, ob=ob: e.tensor_add(
                x1_sb[:, t, 512 * dmg:512 * dmg + 512], ph(ob),
                xbuf2[dmg // 2][:, 512 * (dmg % 2):512 * (dmg % 2) + 512]),
               reads=[r_pb[ob], r_xb2[dmg // 2]], writes=[r_x1[t]],
               alias=([r_kfm, r_vtm, r_qall]
                      if (t == 0 and dmg == 0) else ()))
        if t % 8 < 4:
            em("sync", lambda e, t=t: e.dma_start(
                out=dx1[128 * t:128 * t + 128, :], in_=x1_sb[:, t, :]),
               reads=[r_x1[t]], writes=[r_x1sp[t]], sem="sa")
        if t < 8:
            ln2_stats(0, t)

    for sp in range(NSPAN):
        for tt in range(8):
            ln2_apply(sp, tt)
        def g_dma(i):
            gsl = i % 2
            em("sync", lambda e, i=i, gsl=gsl, sp=sp: e.dma_start(
                out=gf[sp][gsl],
                in_=dgt[i].rearrange("p c h -> p (c h)")),
               writes=[r_gf[gsl]],
               alias=([r_x1[8 * sp + 2 * gsl], r_x1[8 * sp + 2 * gsl + 1]]
                      if i < 2 else ()),
               sem=("ga", "gb")[gsl])

        def g_cast(i):
            gsl = i % 2
            al = ([r_wot[0], r_wot[3], r_pt[0], r_pt[1]] +
                  list(r_ex) + list(r_pr) + [r_dnb[gsl]]
                  if (sp == 0 and i < 2) else
                  ([r_dnb[gsl]] if i < 2 else []))
            em("scalar", lambda e, gsl=gsl, sp=sp: e.copy(
                gub[gsl].rearrange("p c h -> p (c h)"), gf[sp][gsl]),
               reads=[r_gf[gsl]], writes=[r_gub[gsl]], alias=al)

        g_dma(0)
        g_cast(0)
        g_dma(1)
        for i in range(IC):
            gsl = i % 2
            usl = 2 + (i % 2)
            if i + 2 < IC:
                g_dma(i + 2)
            if i + 1 < IC:
                g_cast(i + 1)
            al_g = ([r_wot[0], r_wot[3], r_pt[0], r_pt[1]] +
                    list(r_ex) + list(r_pr)
                    if (sp == 0 and i < 2) else ())
            em("gpsimd", lambda e, i=i, usl=usl: e.dma_start(
                out=gub[usl], in_=dut[i], max_dma_last_dim=8192),
               writes=[r_gub[usl]],
               alias=list(al_g) + [r_osb[usl - 2]], sem=("ua", "ub")[usl - 2])
            for grp in range(2):
                gb = 4 * (i % 2) + grp
                ub = 4 * (i % 2) + 2 + grp
                xres = [r_xn2T[4 * grp + k] for k in range(4)]
                em("tensor", lambda e, grp=grp, gsl=gsl, gb=gb: [
                    e.matmul(ph(gb), lhsT=gub[gsl][:, dc, :],
                             rhs=xn2T[:, dc, 512 * grp:512 * grp + 512],
                             start=(dc == 0), stop=(dc == DC - 1))
                    for dc in range(DC)][-1],
                   reads=xres + [r_gub[gsl]], writes=[r_pb[gb]])
                em("tensor", lambda e, grp=grp, usl=usl, ub=ub: [
                    e.matmul(ph(ub), lhsT=gub[usl][:, dc, :],
                             rhs=xn2T[:, dc, 512 * grp:512 * grp + 512],
                             start=(dc == 0), stop=(dc == DC - 1))
                    for dc in range(DC)][-1],
                   reads=xres + [r_gub[usl]], writes=[r_pb[ub]])
                em("scalar", lambda e, i=i, gb=gb: e.activation(
                    silub, ph(gb), AF.Silu, scale=gs_sb[:, i:i + 1]),
                   reads=[r_pb[gb], r_misc], writes=[r_silu],
                   alias=([r_pt[0], r_pt[1], r_wot[3]]
                          if (sp == 0 and i == 0 and grp == 0) else ()))
                if sp == 0 and i == 0 and grp == 0:
                    hal = ([r_ao[t] for t in range(NT)] +
                           [r_xb2[0], r_xb2[1], r_mk[0], r_mk[1]] +
                           list(r_ex))
                elif sp == 0 and i == 36 and grp == 0:
                    hal = [r_wot[1]]
                else:
                    hal = ()
                em("vector", lambda e, i=i, grp=grp, ub=ub:
                   e.scalar_tensor_tensor(
                       (h_a if i < 36 else h_b)[:, (i if i < 36 else i - 36),
                                                512 * grp:512 * grp + 512],
                       ph(ub), us_sb[:, i:i + 1], silub,
                       ALU.mult, ALU.mult),
                   reads=[r_pb[ub], r_silu, r_misc], writes=[r_h[i]],
                   alias=hal)
        em("sync", lambda e: e.dma_start(out=dsb_sb, in_=ddsb),
           writes=[r_dsb], alias=[r_xn2T[k] for k in range(8)], sem="ds")
        NCH = 4 * 11

        def dn_slot(ci):
            if ci % 2:
                return 1
            return 0 if (ci // 2) % 2 == 0 else 2

        def dn_dma(ci):
            dmg_, i4_ = divmod(ci, 11)
            sl = dn_slot(ci)
            if ci % 2 == 0:
                al = ([r_gub[0]] if ci == 0 else
                      ([r_xnst2] if ci == 2 else ()))
                em("gpsimd", lambda e, dmg_=dmg_, i4_=i4_, sl=sl: e.dma_start(
                    out=dnb[sl], in_=ddnt[dmg_, i4_], max_dma_last_dim=8192),
                   writes=[r_dnb[sl]], alias=al,
                   sem=("ua", "ub")[(ci // 2) % 2])
            else:
                fsl = (ci // 2) % 2
                em("sync", lambda e, dmg_=dmg_, i4_=i4_, fsl=fsl: e.dma_start(
                    out=dnf[fsl],
                    in_=ddnt[dmg_, i4_].rearrange("p k h -> p (k h)")),
                   writes=[r_dnf[fsl]],
                   alias=([r_xn2T[kk] for kk in range(8)] if ci == 1 else ()),
                   sem="ds")

        def dn_cast(ci):
            if ci % 2 == 0:
                return
            fsl = (ci // 2) % 2
            em("scalar", lambda e, fsl=fsl: e.copy(
                dnb[1].rearrange("p k h -> p (k h)"), dnf[fsl]),
               reads=[r_dnf[fsl]], writes=[r_dnb[1]],
               alias=([r_gub[1]] if ci == 1 else ()))

        def x1_reload(d):
            xsl = d % 2
            for tt in range(4):
                em("sync", lambda e, d=d, xsl=xsl, tt=tt, sp=sp: e.dma_start(
                    out=x1r[sp][xsl][:, tt, :],
                    in_=dx1[128 * (8 * sp + tt):128 * (8 * sp + tt) + 128,
                            512 * d:512 * d + 512]),
                   reads=[r_x1sp[8 * sp + tt]], writes=[r_x1r[xsl]],
                   alias=([r_gf[0], r_gf[1]] if tt == 0 else ()),
                   sem=("x2a", "x2b")[xsl])

        dn_dma(0)
        dn_dma(1)
        x1_reload(0)
        for ci in range(NCH):
            dmg, i4 = divmod(ci, 11)
            dsl = dn_slot(ci)
            if ci + 1 < NCH:
                dn_cast(ci + 1)
            if i4 == 5 and dmg + 1 < 4:
                x1_reload(dmg + 1)
            if sp == 0 and 4 <= ci < 36 and ci % 4 == 0:
                ln2_stats(1, (ci - 4) // 4)
            for k in range(4):
                i = 4 * i4 + k
                if i == 0:
                    for tt in range(8):
                        em("tensor", lambda e, tt=tt, dsl=dsl:
                           e.matmul(ph(tt), lhsT=h_ap(0, tt),
                                    rhs=dnb[dsl][:, 0, :],
                                    start=True, stop=False),
                           reads=[r_h[0], r_dnb[dsl]],
                           writes=[r_pb[tt]])
                    continue
                em("tensor", lambda e, i=i, k=k, dsl=dsl: [
                    e.matmul(ph(tt), lhsT=h_ap(i, tt),
                             rhs=dnb[dsl][:, k, :],
                             start=(i == 0), stop=(i == IC - 1))
                    for tt in range(8)],
                   reads=[r_h[i], r_dnb[dsl]],
                   writes=[r_pb[tt] for tt in range(8)], n=8)
            if ci + 2 < NCH:
                dn_dma(ci + 2)
            if i4 != 10:
                continue
            for tt in range(8):
                t = 8 * sp + tt
                oslot = tt % 2
                em("vector", lambda e, t=t, tt=tt, dmg=dmg, oslot=oslot:
                   e.tensor_mul(osb[oslot], ph(tt),
                                dsb_sb[:, 512 * dmg:512 * dmg + 512]),
                   reads=[r_pb[tt], r_dsb], writes=[r_osb[oslot]],
                   alias=([r_gub[2 + oslot]]
                          if (sp == 0 and dmg == 0 and tt < 2) else ()))
                if tt < 4:
                    xsrc = x1r[sp][dmg % 2][:, tt, :]
                    xread = r_x1r[dmg % 2]
                else:
                    xsrc = x1_sb[:, t, 512 * dmg:512 * dmg + 512]
                    xread = r_x1[t]
                em("vector", lambda e, oslot=oslot, xsrc=xsrc:
                   e.tensor_add(osb[oslot], osb[oslot], xsrc),
                   reads=[xread], writes=[r_osb[oslot]])
                em("sync", lambda e, t=t, dmg=dmg, oslot=oslot: e.dma_start(
                    out=dout[128 * t:128 * t + 128,
                             512 * dmg:512 * dmg + 512],
                    in_=osb[oslot]),
                   reads=[r_osb[oslot]], sem=("sa", "sb")[oslot])

    if debug:
        dbg_dump("d_x1", x1_sb, [r_x1[t] for t in range(NT)])

    B.barrier_wait("sync", [("sa", B.P.cnt["sa"]), ("sb", B.P.cnt["sb"])])

    import contextlib
    stack = contextlib.ExitStack()
    sem_handles = {s: stack.enter_context(nc.semaphore(s)) for s in SEMS}
    blk = stack.enter_context(nc.Block())

    def runner(queue):
        def run(eng):
            for fn, waits, inc_sem, inc_n in queue:
                for s, v in waits:
                    eng.wait_ge(sem_handles[s], v)
                if fn is None:
                    continue
                inst = fn(eng)
                if isinstance(inst, list):
                    assert len(inst) == inc_n, (len(inst), inc_n)
                    for i_ in inst:
                        i_.then_inc(sem_handles[inc_sem], 1)
                elif inc_sem is not None:
                    inst.then_inc(sem_handles[inc_sem], inc_n)
        return run

    blk.tensor(runner(B.P.q["tensor"]))
    blk.vector(runner(B.P.q["vector"]))
    blk.scalar(runner(B.P.q["scalar"]))
    blk.sync(runner(B.P.q["sync"]))
    blk.gpsimd(runner(B.P.q["gpsimd"]))
    stack.close()
    return nc


_NC_CACHE = None


def kernel(L0_attn_x, L0_attn_wq, L0_attn_wk, L0_attn_wv, L0_attn_wo,
           L0_ffn_gate_w, L0_ffn_up_w, L0_ffn_down_w, L0_ffn_gate_s,
           L0_ffn_up_s, L0_ffn_down_s):
    global _NC_CACHE
    x = np.asarray(L0_attn_x, dtype=np.float32)
    wq = np.asarray(L0_attn_wq, np.float32)
    wk = np.asarray(L0_attn_wk, np.float32)
    wv = np.asarray(L0_attn_wv, np.float32)
    wo = np.asarray(L0_attn_wo, np.float32)
    gw = np.asarray(L0_ffn_gate_w, np.float32)
    uw = np.asarray(L0_ffn_up_w, np.float32)
    dw = np.asarray(L0_ffn_down_w, np.float32)
    gs = np.asarray(L0_ffn_gate_s, np.float32)
    us = np.asarray(L0_ffn_up_s, np.float32)
    ds = np.asarray(L0_ffn_down_s, np.float32)

    shared = {
        "wqt": _prep_lhsT(wq, 16),
        "wkt": _prep_lhsT(wk, 4),
        "wvt": _prep_wvt(wv),
        "wot": _prep_wot(wo),
        "gt": _prep_lhsT(gw, IC),
        "ut": _prep_lhsT(uw, IC),
        "dnt": _prep_dnt(dw),
        "gsc": np.ascontiguousarray(gs.reshape(IC, 128).T),
        "usc": np.ascontiguousarray(us.reshape(IC, 128).T),
        "dsb": np.ascontiguousarray(np.broadcast_to(ds, (128, D))),
        "ide": np.eye(128, dtype=np.float32),
        "cst": np.ascontiguousarray(np.broadcast_to(
            np.array([1.0 / D, EPS, -1.0, 0.0], np.float32), (128, 4))),
    }
    in_maps = []
    for core in range(8):
        s, h = core // 2, core % 2
        m = dict(shared)
        m["xo"] = np.ascontiguousarray(x[s, h * TOK:(h + 1) * TOK])
        m["xh"] = np.ascontiguousarray(
            x[s, 1792:2048] if h == 1 else x[s, 0:HALO])
        m["xs"] = np.ascontiguousarray(x[s, 0:SINK])
        m["mask"] = _make_mask(core)
        in_maps.append(m)

    if _NC_CACHE is None:
        _NC_CACHE = build_nc()
    res = run_bass_kernel_spmd(_NC_CACHE, in_maps, core_ids=list(range(8)))
    out = np.empty((4, 4096, D), dtype=np.float32)
    for core in range(8):
        s, h = core // 2, core % 2
        out[s, h * TOK:(h + 1) * TOK] = res.results[core]["out"]
    return out


# version 33
# speedup vs baseline: 1.0444x; 1.0324x over previous
import math

import ml_dtypes
import numpy as np

import concourse.bass as bass
import concourse.mybir as mybir
from concourse.bass_utils import run_bass_kernel_spmd

F32 = mybir.dt.float32
BF16 = mybir.dt.bfloat16
AF = mybir.ActivationFunctionType
ALU = mybir.AluOpType
AX = mybir.AxisListType

D = 2048
DC = 16
NH = 16
NKV = 4
HD = 128
INTER = 5632
IC = 44
TOK = 2048
NT = 16
HALO = 256
SINK = 4
BUF_T = HALO + TOK + SINK
NBT = 19
BT_PAD = 2320
NQI = 16
SPAN = 1024
NSPAN = 2
SC_N = 388
NEG = -1.0e30
QSCALE = 1.0 / math.sqrt(HD)
EPS = 1e-5

_DMA_SEMS_L = ["c", "cw", "xa", "xb", "xc", "ma", "mb", "x2a", "x2b", "qa", "qb",
               "wo0", "wo1", "wo2", "wo3", "ga", "gb", "ua", "ub",
               "ds", "sa", "sb"]
SEMS = tuple(["t", "v", "s"] + _DMA_SEMS_L)
DMA_SEMS = tuple(_DMA_SEMS_L)
ENG_SEM = {"tensor": "t", "vector": "v", "scalar": "s"}
ENGS = ("tensor", "vector", "scalar", "sync", "gpsimd")


def _make_mask(core):
    h = core % 2
    base = h * TOK
    qg = base + (128 * np.arange(NQI))[:, None, None] + \
        np.arange(128)[None, :, None]
    kb = 128 * np.arange(NQI)[:, None, None] + np.arange(384)[None, None, :]
    if h == 1:
        kg = np.where(kb < HALO, 1792 + kb, base + (kb - HALO))
        valid = np.ones(np.broadcast_shapes(qg.shape, kb.shape), dtype=bool)
    else:
        kg = np.where(kb < HALO, 0, base + (kb - HALO))
        valid = np.broadcast_to(kb >= HALO,
                                np.broadcast_shapes(qg.shape, kb.shape))
    ok = valid & (kg <= qg) & ((kg < SINK) | (kg >= qg - 255))
    m = np.full((NQI, 128, SC_N), NEG, dtype=np.float32)
    m[:, :, :384] = np.where(ok, 0.0, NEG)
    for qi in range(NQI):
        for j in range(SINK):
            if h == 0 and (128 * qi) <= (HALO + j) < (128 * qi + 384):
                continue
            qcol = base + 128 * qi + np.arange(128)
            m[qi, :, 384 + j] = np.where(j <= qcol, 0.0, NEG)
    return m


def _prep_lhsT(w, oc):
    return np.ascontiguousarray(w.reshape(oc, 128, DC, 128).transpose(0, 3, 2, 1))


def _prep_wvt(wv):
    return np.ascontiguousarray(
        np.ascontiguousarray(wv.T).reshape(DC, 128, 512).transpose(1, 0, 2))


def _prep_wot(wo):
    return np.ascontiguousarray(wo.reshape(4, 512, 16, 128).transpose(0, 3, 2, 1))


def _prep_dnt(dw):
    t = np.ascontiguousarray(dw.T).reshape(11, 4, 128, 4, 512)
    return np.ascontiguousarray(t.transpose(3, 0, 2, 1, 4))


class Prog:
    def __init__(self):
        self.q = {e: [] for e in ENGS}
        self.cnt = {s: 0 for s in SEMS}

    def op(self, eng, fn, waits, inc_sem, inc_n):
        w = {}
        own = ENG_SEM.get(eng)
        for tok in waits:
            if tok is None:
                continue
            s, v = tok
            if s in DMA_SEMS:
                v = self.cnt[s]
            if w.get(s, 0) < v:
                w[s] = v
        tok = None
        if inc_sem is not None:
            if inc_sem in DMA_SEMS and self.cnt[inc_sem] > 0:
                if w.get(inc_sem, 0) < self.cnt[inc_sem]:
                    w[inc_sem] = self.cnt[inc_sem]
            self.cnt[inc_sem] += inc_n
            tok = (inc_sem, self.cnt[inc_sem])
        self.q[eng].append((fn, sorted(w.items()), inc_sem, inc_n))
        return tok


class Res:
    __slots__ = ("w", "rs")

    def __init__(self):
        self.w = None
        self.rs = []


class Build:
    def __init__(self):
        self.P = Prog()

    def emit(self, eng, fn, reads=(), writes=(), alias=(), sem=None, n=None):
        waits = []
        for r in reads:
            waits.append(r.w)
        for r in writes:
            waits.append(r.w)
            waits.extend(r.rs)
        for r in alias:
            waits.append(r.w)
            waits.extend(r.rs)
        inc = sem if sem is not None else ENG_SEM[eng]
        inc_n = n if n is not None else (16 if inc in DMA_SEMS else 1)
        t = self.P.op(eng, fn, waits, inc, inc_n)
        for r in reads:
            r.rs.append(t)
        for r in writes:
            r.w = t
            r.rs = []
        return t

    def barrier_wait(self, eng, tokens):
        self.P.op(eng, None, [t for t in tokens if t], None, 0)


def build_nc(debug=False):
    DEBUG = [debug]
    nc = bass.Bass("TRN2", target_bir_lowering=False)
    B = Build()
    em = B.emit

    dxo = nc.dram_tensor("xo", [TOK, D], F32, kind="ExternalInput").ap()
    dxh = nc.dram_tensor("xh", [HALO, D], F32, kind="ExternalInput").ap()
    dxs = nc.dram_tensor("xs", [SINK, D], F32, kind="ExternalInput").ap()
    dmask = nc.dram_tensor("mask", [NQI, 128, SC_N], F32, kind="ExternalInput").ap()
    dwqt = nc.dram_tensor("wqt", [16, 128, DC, 128], F32, kind="ExternalInput").ap()
    dwkt = nc.dram_tensor("wkt", [4, 128, DC, 128], F32, kind="ExternalInput").ap()
    dwvt = nc.dram_tensor("wvt", [128, DC, 512], F32, kind="ExternalInput").ap()
    dwot = nc.dram_tensor("wot", [4, 128, 16, 512], F32, kind="ExternalInput").ap()
    dgt = nc.dram_tensor("gt", [IC, 128, DC, 128], F32, kind="ExternalInput").ap()
    dut = nc.dram_tensor("ut", [IC, 128, DC, 128], F32, kind="ExternalInput").ap()
    ddnt = nc.dram_tensor("dnt", [4, 11, 128, 4, 512], F32, kind="ExternalInput").ap()
    dgs = nc.dram_tensor("gsc", [128, IC], F32, kind="ExternalInput").ap()
    dus = nc.dram_tensor("usc", [128, IC], F32, kind="ExternalInput").ap()
    ddsb = nc.dram_tensor("dsb", [128, D], F32, kind="ExternalInput").ap()
    dide = nc.dram_tensor("ide", [128, 128], F32, kind="ExternalInput").ap()
    dcst = nc.dram_tensor("cst", [128, 4], F32, kind="ExternalInput").ap()
    dout = nc.dram_tensor("out", [TOK, D], F32, kind="ExternalOutput").ap()
    dx1 = nc.dram_tensor("x1spill", [TOK, D], BF16, kind="Internal").ap()

    SB_BASE = 16512

    def sb(name, off, shape, dtype):
        import math as _m
        nb = off + SB_BASE + int(np.prod(shape[1:])) * (
            4 if dtype == F32 else 2)
        assert nb <= 229376, (name, nb)
        return nc.alloc_sbuf_tensor_at(name, shape, dtype,
                                       offset=off + SB_BASE).ap()

    idf = sb("idf", 0, [128, 128], F32)
    stats = sb("stats", 512, [128, 24], F32)
    gs_sb = sb("gs_sb", 608, [128, IC], F32)
    us_sb = sb("us_sb", 800, [128, IC], F32)
    cst_sb = sb("cst_sb", 992, [128, 4], F32)

    xnT = sb("xnT", 1024, [128, DC, BT_PAD], BF16)
    aoall = sb("aoall", 1024, [128, NH, TOK], BF16)
    xbuf2 = [sb(f"xb2_{i}", 66560 + 4096 * i, [128, 1024], F32)
             for i in range(2)]
    h_a = sb("h_a", 1024, [128, 36, SPAN], BF16)

    kfm = sb("kfm", 75264, [128, NKV, BT_PAD], BF16)
    vtm = sb("vtm", 93824, [128, NBT, NKV, HD], BF16)
    x1_sb = sb("x1", 75264, [128, NT, D], BF16)

    qall = sb("qall", 113280, [128, NH, TOK], BF16)
    xbuf = [sb(f"xb_{i}", off, [128, D], F32)
            for i, off in enumerate((113280, 121472, 178816))]
    xnst = [sb(f"xnst_{i}", 129664 + 8192 * i, [128, D], F32)
            for i in range(2)]
    wkt_sb = sb("wkt_sb", 146048, [128, 4, DC, 128], BF16)
    wvt_sb = sb("wvt_sb", 162432, [128, DC, 512], BF16)

    wqb = [sb(f"wqb_{i}", 178816 + 4096 * i, [128, DC, 128], BF16)
           for i in range(2)]
    wot_sb = [
        sb("wot0", 178816, [128, 16, 512], BF16),
        sb("wot1", 140800, [128, 16, 512], BF16),
        sb("wot2", 157184, [128, 16, 512], BF16),
        sb("wot3", 195200, [128, 16, 512], BF16),
    ]

    exb = [sb(f"ex_{i}", 69696 + 800 * i, [128, 392], BF16)
           for i in range(4)]
    prb = [sb(f"pr_{i}", 198336 + 1568 * i, [128, 392], F32)
           for i in range(4)]
    ptb = [sb(f"pt_{i}", 204608 + 4096 * i, [128, 4, 4, 128], BF16)
           for i in range(2)]
    mkb = [sb(f"mk_{i}", 66560 + 1568 * i, [128, SC_N], F32)
           for i in range(2)]

    h_b = sb("h_b", 140800, [128, 8, SPAN], BF16)
    xn2T = sb("xn2T", 157184, [128, DC, SPAN], BF16)
    dsb_sb = sb("dsb_sb", 157184, [128, D], F32)
    gub = [sb(f"gub_{i}", 189952 + 4096 * i, [128, DC, 128], BF16)
           for i in range(4)]
    dnb = [sb(f"dnb_{i}", 189952 + 4096 * i, [128, 4, 512], BF16)
           for i in range(2)]
    dnb.append(sb("dnb_2", 206336, [128, 4, 512], BF16))
    dnf = [sb(f"dnf_{i}", off, [128, D], F32)
           for i, off in enumerate((165376, 177664))]
    gf = [[sb(f"gf_{s}{j}", 75264 + 32768 * s + 8192 * j, [128, D], F32)
           for j in range(2)] for s in range(2)]
    x1r = [[sb(f"x1r_{s}{j}", 75264 + 32768 * s + 4096 * j,
               [128, 4, 512], BF16) for j in range(2)]
           for s in range(2)]
    osb = [sb(f"osb_{i}", 198144 + 4096 * i, [128, 512], F32)
           for i in range(2)]
    xnst2 = sb("xnst2", 206336, [128, 1024], F32)
    lnjunk = sb("lnjunk", 173568, [128, 1024], F32)
    ajunk = sb("ajunk", 187008, [128, D], F32)
    silub = sb("silub", 210432, [128, 512], F32)

    pbt = [nc.alloc_psum_tensor(f"pb{i}", [128, 1024], F32).ap()
           for i in range(4)]

    def ph(i):
        return pbt[i // 2][:, 512 * (i % 2):512 * (i % 2) + 512]

    def dbg_dump(nm, ap_, rr):
        dd = nc.dram_tensor(nm, list(ap_.shape), ap_.dtype,
                            kind="ExternalOutput").ap()
        em("sync", lambda e, dd=dd, ap_=ap_: e.dma_start(out=dd, in_=ap_),
           reads=list(rr), sem="sa")

    r_pb = [Res() for _ in range(8)]
    r_xnT = [Res() for _ in range(NBT)]
    r_kfm, r_vtm, r_qall = Res(), Res(), Res()
    r_wkt, r_wvt = Res(), Res()
    r_wqb = [Res(), Res()]
    r_wot = [Res() for _ in range(4)]
    r_xbuf = [Res(), Res(), Res()]
    r_xnst = [Res(), Res()]
    r_xb2 = [Res(), Res()]
    r_ex = [Res() for _ in range(4)]
    r_pr = [Res() for _ in range(4)]
    r_pt = [Res(), Res()]
    r_mk = [Res(), Res()]
    r_ao = [Res() for _ in range(NT)]
    r_x1 = [Res() for _ in range(NT)]
    r_h = [Res() for _ in range(IC)]
    r_xn2T = [Res() for _ in range(8)]
    r_gub = [Res() for _ in range(4)]
    r_dnb = [Res(), Res(), Res()]
    r_dnf = [Res(), Res()]
    r_gf = [Res(), Res()]
    r_x1r = [Res(), Res()]
    r_x1sp = [Res() for _ in range(NT)]
    r_osb = [Res(), Res()]
    r_stats = [Res(), Res()]
    r_bst = [Res() for _ in range(4)]
    r_xnst2 = Res()
    r_lnj = Res()
    r_ajk = Res()
    r_red = [Res(), Res()]
    r_silu = Res()
    r_dsb = Res()
    r_misc = Res()

    nc.const_aps.register(F32, 1.0 / D, cst_sb[:, 0:1])
    nc.const_aps.register(F32, EPS, cst_sb[:, 1:2])
    nc.const_aps.register(F32, -1.0, cst_sb[:, 2:3])

    em("sync", lambda e: e.dma_start(out=cst_sb, in_=dcst), writes=[r_misc], sem="c")
    em("sync", lambda e: e.dma_start(out=idf, in_=dide), writes=[r_misc], sem="c")
    em("sync", lambda e: e.dma_start(out=gs_sb, in_=dgs), writes=[r_misc], sem="c")
    em("sync", lambda e: e.dma_start(out=us_sb, in_=dus), writes=[r_misc], sem="c")

    em("gpsimd", lambda e: e.dma_start(
        out=wkt_sb, in_=dwkt.rearrange("h p c i -> p h c i"),
        max_dma_last_dim=8192), writes=[r_wkt], sem="cw")
    em("gpsimd", lambda e: e.dma_start(
        out=wvt_sb, in_=dwvt, max_dma_last_dim=8192), writes=[r_wvt], sem="cw")

    def ln_front(src_ap, rows, read_res, sset, alias1=()):
        st = stats[0:rows, 8 * sset:8 * sset + 8]
        s_nm, s_sq = st[:, 0:1], st[:, 1:2]
        rs = [read_res] if read_res is not None else []
        em("vector", lambda e: e.reduce_sum(s_nm, src_ap, AX.X, negate=True),
           reads=rs, writes=[r_red[sset]])
        em("scalar", lambda e: e.activation(ajunk[0:rows, :], src_ap,
                                            AF.Square, accum_out=s_sq),
           reads=rs, writes=[r_stats[sset], r_ajk], alias=alias1)

    def ln_back(src_ap, rows, xout, xout_res, read_res, sset):
        st = stats[0:rows, 8 * sset:8 * sset + 8]
        s_nm, s_sq, s_mn, s_m2, s_ve, s_var, s_sd, s_rs = (
            st[:, i:i + 1] for i in range(8))
        s_nb = s_var
        rs = [read_res] if read_res is not None else []
        em("vector", lambda e: e.tensor_scalar_mul(s_mn, s_nm, 1.0 / D),
           reads=[r_red[sset]], writes=[r_stats[sset]])
        em("vector", lambda e: e.tensor_scalar(s_ve, s_sq, 1.0 / D, EPS,
                                               ALU.mult, ALU.add),
           writes=[r_stats[sset]])
        em("vector", lambda e: e.tensor_mul(s_m2, s_mn, s_mn),
           writes=[r_stats[sset]])
        em("vector", lambda e: e.tensor_sub(s_var, s_ve, s_m2),
           writes=[r_stats[sset]])
        em("scalar", lambda e: e.activation(s_sd, s_var, AF.Sqrt),
           writes=[r_stats[sset]])
        em("vector", lambda e: e.reciprocal(s_rs, s_sd),
           writes=[r_stats[sset]])
        em("vector", lambda e: e.tensor_mul(s_nb, s_mn, s_rs),
           writes=[r_stats[sset]])
        em("vector", lambda e: e.tensor_scalar(xout, src_ap, s_rs, s_nb,
                                               ALU.mult, ALU.add),
           reads=rs + [r_stats[sset]], writes=[xout_res])

    def transpose_tile(xn_ap, xn_res, rows, dst, dst_res, dst_col,
                       extra_alias=()):
        for hf in range(4):
            bank = 4 + hf
            em("tensor", lambda e, hf=hf, rw=rows: [
                e.transpose(out=ph(4 + hf)[:, 128 * c:128 * c + rw],
                            in_=xn_ap[:, 128 * (4 * hf + c):
                                      128 * (4 * hf + c) + 128],
                            identity=idf[0:rw, 0:rw])
                for c in range(4)],
               reads=[xn_res, r_misc], writes=[r_pb[bank]], n=4)
            evac_eng = "scalar" if hf % 2 == 0 else "vector"

            def _evac(e, hf=hf, rw=rows):
                view = ph(4 + hf).rearrange("p (c q) -> p c q", c=4)[:, :, 0:rw]
                dstv = dst[:, 4 * hf:4 * hf + 4, dst_col:dst_col + rw]
                return (e.copy(dstv, view) if hf % 2 == 0
                        else e.tensor_copy(dstv, view))
            em(evac_eng, _evac,
               reads=[r_pb[bank]], writes=[dst_res], alias=extra_alias)

    def a_src(b):
        if b < 2:
            return dxh[128 * b:128 * b + 128]
        if b < 18:
            return dxo[128 * (b - 2):128 * (b - 2) + 128]
        return dxs

    def a_tr_stage(b):
        rows = 4 if b == NBT - 1 else 128
        sset = b % 2
        transpose_tile(xnst[sset][0:rows, :], r_xnst[sset], rows,
                       xnT, r_xnT[b], 128 * b)

    def a_proj_stage(b):
        rows = 4 if b == NBT - 1 else 128
        vb = b % 2
        em("tensor", lambda e, b=b, rw=rows, vb=vb: [
            e.matmul(ph(vb)[0:rw, :],
                     lhsT=xnT[:, dc, 128 * b:128 * b + rw],
                     rhs=wvt_sb[:, dc, :],
                     start=(dc == 0), stop=(dc == DC - 1))
            for dc in range(DC)][-1],
           reads=[r_xnT[b], r_wvt], writes=[r_pb[vb]])
        em("scalar", lambda e, b=b, rw=rows, vb=vb: e.copy(
            vtm[0:rw, b, :, :],
            ph(vb)[0:rw, :].rearrange("p (k h) -> p k h", k=NKV)),
           reads=[r_pb[vb]], writes=[r_vtm])
        if b % 4 == 3 or b == NBT - 1:
            g = b // 4
            g0 = 512 * g
            ng = min(512, BUF_T - g0)
            tiles = [r_xnT[tt] for tt in range(4 * g, min(4 * g + 4, NBT))]
            for hc in range(4):
                kb = 2 + (hc % 2)
                em("tensor", lambda e, hc=hc, g0=g0, ng=ng, kb=kb: [
                    e.matmul(ph(kb)[:, 0:ng],
                             lhsT=wkt_sb[:, hc, dc, :],
                             rhs=xnT[:, dc, g0:g0 + ng],
                             start=(dc == 0), stop=(dc == DC - 1))
                    for dc in range(DC)][-1],
                   reads=tiles + [r_wkt], writes=[r_pb[kb]])
                em("scalar", lambda e, hc=hc, g0=g0, ng=ng, kb=kb: e.copy(
                    kfm[:, hc, g0:g0 + ng], ph(kb)[:, 0:ng]),
                   reads=[r_pb[kb]], writes=[r_kfm])

    for b in range(NBT + 3):
        if b < NBT:
            rows = 4 if b == NBT - 1 else 128
            sl = b % 3
            em("sync", lambda e, s=a_src(b), sl=sl, rw=rows: e.dma_start(
                out=xbuf[sl][0:rw, :], in_=s), writes=[r_xbuf[sl]],
               sem=("xa", "xb", "xc")[sl])
            ln_front(xbuf[sl][0:rows, :], rows, r_xbuf[sl], b % 2)
        if 1 <= b <= NBT:
            bb = b - 1
            rows = 4 if bb == NBT - 1 else 128
            sl = bb % 3
            sset = bb % 2
            ln_back(xbuf[sl][0:rows, :], rows, xnst[sset][0:rows, :],
                    r_xnst[sset], r_xbuf[sl], sset)
        if 2 <= b <= NBT + 1:
            a_tr_stage(b - 2)
        if 3 <= b:
            a_proj_stage(b - 3)

    for hc in range(16):
        sl = hc % 2
        em("gpsimd", lambda e, hc=hc, sl=sl: e.dma_start(
            out=wqb[sl], in_=dwqt[hc], max_dma_last_dim=8192),
           writes=[r_wqb[sl]],
           alias=([r_xbuf[2]] if hc < 2 else ()),
           sem=("qa", "qb")[sl])
        for g in range(4):
            qb = 4 + ((hc * 4 + g) % 2)
            tiles = [r_xnT[2 + 4 * g + i] for i in range(4)]
            em("tensor", lambda e, hc=hc, g=g, sl=sl, qb=qb: [
                e.matmul(ph(qb),
                         lhsT=wqb[sl][:, dc, :],
                         rhs=xnT[:, dc, HALO + 512 * g:HALO + 512 * g + 512],
                         start=(dc == 0), stop=(dc == DC - 1))
                for dc in range(DC)][-1],
               reads=tiles + [r_wqb[sl]], writes=[r_pb[qb]])
            em("vector", lambda e, hc=hc, g=g, qb=qb: e.tensor_scalar_mul(
                qall[:, hc, 512 * g:512 * g + 512], ph(qb), QSCALE),
               reads=[r_pb[qb]], writes=[r_qall],
               alias=([r_xbuf[0], r_xbuf[1], r_xnst[0], r_xnst[1],
                       r_wkt, r_wvt] if (hc == 0 and g == 0) else ()))

    if debug:
        dbg_dump("d_xnT", xnT, r_xnT)

    em("gpsimd", lambda e: e.dma_start(out=wot_sb[0], in_=dwot[0],
                                       max_dma_last_dim=8192),
       writes=[r_wot[0]], alias=[r_wqb[0], r_wqb[1], r_ajk], sem="wo0")

    st_neg = [stats[:, 8 + i:9 + i] for i in range(4)]
    st_sum = [stats[:, 12 + i:13 + i] for i in range(4)]
    st_den = [stats[:, 16 + i:17 + i] for i in range(4)]
    st_rec = [stats[:, 20 + i:21 + i] for i in range(4)]

    def sm_scores(qi, kv, j):
        head = 4 * kv + j
        sbk = j
        em("tensor", lambda e, qi=qi, head=head, sbk=sbk: [
            e.matmul(ph(sbk)[:, 0:384],
                     lhsT=qall[:, head, 128 * qi:128 * qi + 128],
                     rhs=kfm[:, head // 4, 128 * qi:128 * qi + 384],
                     start=True, stop=True),
            e.matmul(ph(sbk)[:, 384:SC_N],
                     lhsT=qall[:, head, 128 * qi:128 * qi + 128],
                     rhs=kfm[:, head // 4, 2304:2308],
                     start=True, stop=True)],
           reads=[r_qall, r_kfm], writes=[r_pb[sbk]], n=2)

    def sm_stats(qi, kv, j):
        sbk = j
        al0 = ([r_stats[0], r_stats[1], r_red[0], r_red[1]]
               if (qi == 0 and kv == 0 and j == 0) else ())
        em("vector", lambda e, qi=qi, sbk=sbk: e.tensor_add(
            ph(sbk)[:, 0:SC_N], ph(sbk)[:, 0:SC_N], mkb[qi % 2]),
           reads=[r_mk[qi % 2]], writes=[r_pb[sbk]])
        em("vector", lambda e, sbk=sbk, j=j: e.reduce_max(
            st_neg[j], ph(sbk)[:, 0:SC_N], AX.X, negate=True),
           reads=[r_pb[sbk]], writes=[r_bst[j]], alias=al0)

    def sm_exp(qi, kv, j):
        sbk = j
        em("scalar", lambda e, sbk=sbk, j=j: e.activation(
            exb[j][:, 0:SC_N], ph(sbk)[:, 0:SC_N], AF.Exp,
            bias=st_neg[j], accum_out=st_sum[j]),
           reads=[r_pb[sbk], r_bst[j]], writes=[r_ex[j], r_bst[j]],
           alias=([r_xnT[tt] for tt in range(NBT)]
                  if (qi == 0 and kv == 0) else ()))

    def sm_norm(qi, kv, j):
        em("vector", lambda e, j=j: e.tensor_scalar_max(
            st_den[j], st_sum[j], 1e-9), writes=[r_bst[j]])
        em("vector", lambda e, j=j: e.reciprocal(
            st_rec[j], st_den[j]), writes=[r_bst[j]])
        em("vector", lambda e, j=j: e.tensor_scalar_mul(
            prb[j][:, 0:SC_N], exb[j][:, 0:SC_N], st_rec[j]),
           reads=[r_ex[j], r_bst[j]], writes=[r_pr[j]])

    def transpose_head(qi, kv, j):
        it = qi * 4 + kv
        psl = it % 2
        pslot = j
        tb = 6 + (j % 2)
        em("tensor", lambda e, pslot=pslot, tb=tb: [
            e.transpose(out=ph(tb)[:, 128 * s:128 * s + 128],
                        in_=prb[pslot][:, 128 * s:128 * s + 128],
                        identity=idf)
            for s in range(3)] + [
            e.transpose(out=ph(tb)[0:4, 384:512],
                        in_=prb[pslot][:, 384:SC_N],
                        identity=idf)],
           reads=[r_pr[pslot], r_misc], writes=[r_pb[tb]], n=4)
        em("scalar", lambda e, j=j, tb=tb, psl=psl: [
            e.copy(ptb[psl][:, 0:3, j, :],
                   ph(tb)[:, 0:384].rearrange("p (s q) -> p s q", s=3)),
            e.copy(ptb[psl][0:4, 3, j, :],
                   ph(tb)[0:4, 384:512])],
           reads=[r_pb[tb]], writes=[r_pt[psl]], n=2)

    def pv_evac(qi, kv):
        it = qi * 4 + kv
        psl = it % 2
        ab = 4 + (it % 2)
        em("tensor", lambda e, qi=qi, kv=kv, psl=psl, ab=ab: [
            e.matmul(ph(ab), lhsT=vtm[:, qi + s, kv, :],
                     rhs=ptb[psl][:, s, :, :], start=(s == 0), stop=False)
            for s in range(3)] + [
            e.matmul(ph(ab), lhsT=vtm[0:4, 18, kv, :],
                     rhs=ptb[psl][0:4, 3, :, :], start=False, stop=True)],
           reads=[r_pt[psl], r_vtm], writes=[r_pb[ab]], n=4)
        if DEBUG[0] and qi == 4 and kv == 0:
            dbg_dump("d_pt", ptb[psl], [r_pt[psl]])
        em("scalar", lambda e, qi=qi, kv=kv, ab=ab: e.copy(
            aoall[:, 4 * kv:4 * kv + 4, 128 * qi:128 * qi + 128],
            ph(ab).rearrange("p (j q) -> p j q", j=4)),
           reads=[r_pb[ab]], writes=[r_ao[qi]],
           alias=([r_xnT[t] for t in range(NBT)]
                  if (qi == 0 and kv == 0) else ()))

    prev = None
    for qi in range(NQI):
        em("sync", lambda e, qi=qi: e.dma_start(out=mkb[qi % 2],
                                                in_=dmask[qi]),
           writes=[r_mk[qi % 2]],
           alias=([r_xnT[tt] for tt in range(NBT)] if qi < 2 else ()),
           sem=("ma", "mb")[qi % 2])
        for kv in range(NKV):
            for j in range(4):
                if prev is not None:
                    transpose_head(prev[0], prev[1], j)
                sm_scores(qi, kv, j)
            for j in range(4):
                sm_stats(qi, kv, j)
            for j in range(4):
                sm_exp(qi, kv, j)
            for j in range(4):
                sm_norm(qi, kv, j)
            if prev is not None:
                pv_evac(*prev)
            prev = (qi, kv)
    for j in range(4):
        transpose_head(prev[0], prev[1], j)
    pv_evac(*prev)

    for g in range(1, 4):
        al = [r_qall] if g in (1, 2) else \
            list(r_ex) + list(r_pr) + [r_pt[0], r_pt[1]]
        em("gpsimd", lambda e, g=g: e.dma_start(
            out=wot_sb[g], in_=dwot[g], max_dma_last_dim=8192),
           writes=[r_wot[g]], alias=al, sem=f"wo{g}")

    if debug:
        dbg_dump("d_kfm", kfm, [r_kfm])
        dbg_dump("d_vtm", vtm, [r_vtm])
        dbg_dump("d_qall", qall, [r_qall])
        dbg_dump("d_ao", aoall, [r_ao[t] for t in range(NT)])

    def h_ap(i, tt):
        buf, ii = (h_a, i) if i < 36 else (h_b, i - 36)
        return buf[:, ii, 128 * tt:128 * tt + 128]

    def ln2_stats(sp, tt):
        t = 8 * sp + tt
        w = stats[:, 16:24]
        w_nm, w_sq, w_sq2, w_sqt, w_mn, w_m2, w_sd, w_ve = (
            w[:, i:i + 1] for i in range(8))
        w_var = w_sq
        f_rs = stats[:, 2 * tt:2 * tt + 1]
        f_nb = stats[:, 2 * tt + 1:2 * tt + 2]
        al1 = (list(r_bst) if (sp == 0 and tt == 0) else ())
        alj = ([r_qall] if (sp == 0 and tt == 0) else
               ([r_xn2T[k] for k in range(8)] if (sp == 1 and tt == 0)
                else ()))
        em("vector", lambda e: e.reduce_sum(w_nm, x1_sb[:, t, :], AX.X,
                                            negate=True),
           reads=[r_x1[t]], writes=[r_bst[0]], alias=al1)
        em("scalar", lambda e: e.activation(
            lnjunk, x1_sb[:, t, 0:1024], AF.Square, accum_out=w_sq),
           reads=[r_x1[t]], writes=[r_bst[0], r_lnj], alias=alj)
        em("scalar", lambda e: e.activation(
            lnjunk, x1_sb[:, t, 1024:2048], AF.Square, accum_out=w_sq2),
           reads=[r_x1[t]], writes=[r_bst[0], r_lnj])
        em("scalar", lambda e: e.activation(w_mn, w_nm, AF.Identity,
                                            scale=1.0 / D),
           reads=[r_misc], writes=[r_bst[0]])
        em("scalar", lambda e: e.activation(w_sqt, w_sq, AF.Identity,
                                            bias=w_sq2),
           writes=[r_bst[0]])
        em("scalar", lambda e: e.activation(w_m2, w_mn, AF.Square),
           writes=[r_bst[0]])
        em("scalar", lambda e: e.activation(w_ve, w_sqt, AF.Identity,
                                            scale=1.0 / D, bias=EPS),
           writes=[r_bst[0]])
        em("scalar", lambda e: e.activation(w_var, w_m2, AF.Identity,
                                            scale=-1.0, bias=w_ve),
           writes=[r_bst[0]])
        em("scalar", lambda e: e.activation(w_sd, w_var, AF.Sqrt),
           writes=[r_bst[0]])
        em("vector", lambda e: e.reciprocal(f_rs, w_sd),
           reads=[r_bst[0]], writes=[r_stats[0], r_stats[1]])
        em("scalar", lambda e: e.activation(f_nb, w_mn, AF.Identity,
                                            scale=f_rs),
           reads=[r_bst[0]], writes=[r_stats[0], r_stats[1]])

    def ln2_apply(sp, tt):
        t = 8 * sp + tt
        f_rs = stats[:, 2 * tt:2 * tt + 1]
        f_nb = stats[:, 2 * tt + 1:2 * tt + 2]
        for hh in range(2):
            em("scalar", lambda e, hh=hh: e.activation(
                xnst2, x1_sb[:, t, 1024 * hh:1024 * hh + 1024],
                AF.Identity, bias=f_nb, scale=f_rs),
               reads=[r_x1[t], r_stats[0], r_stats[1]], writes=[r_xnst2],
               alias=([r_pt[0], r_pt[1], r_wot[3]]
                      if (sp == 0 and tt == 0 and hh == 0) else
                      ([r_dnb[2]] if (tt == 0 and hh == 0) else ())))
            for sub in range(2):
                bank = 4 + 2 * hh + sub
                em("tensor", lambda e, sub=sub, bank=bank: [
                    e.transpose(out=ph(bank)[:, 128 * c:128 * c + 128],
                                in_=xnst2[:, 128 * (4 * sub + c):
                                          128 * (4 * sub + c) + 128],
                                identity=idf)
                    for c in range(4)],
                   reads=[r_xnst2, r_misc], writes=[r_pb[bank]], n=4)
                dc0 = 8 * hh + 4 * sub
                em("vector", lambda e, dc0=dc0, bank=bank, tt=tt:
                   e.tensor_copy(
                       xn2T[:, dc0:dc0 + 4, 128 * tt:128 * tt + 128],
                       ph(bank).rearrange("p (c q) -> p c q", c=4)),
                   reads=[r_pb[bank]], writes=[r_xn2T[tt]],
                   alias=([r_wot[2], r_wot[0], r_qall, r_dsb, r_lnj,
                           r_dnf[0], r_dnf[1]]
                          if (tt == 0 and hh == 0 and sub == 0) else ()))

    for t in range(NT):
        for half in range(2):
            em("sync", lambda e, t=t, half=half: e.dma_start(
                out=xbuf2[half],
                in_=dxo[128 * t:128 * t + 128,
                        1024 * half:1024 * half + 1024]),
               writes=[r_xb2[half]],
               alias=(([r_xnT[tt] for tt in range(NBT)] +
                       [r_mk[0], r_mk[1]] + list(r_ex)) if t == 0 else ()),
               sem=("x2a", "x2b")[half])
        for dmg in range(4):
            ob = dmg if t % 2 == 0 else 4 + dmg
            em("tensor", lambda e, t=t, dmg=dmg, ob=ob: [
                e.matmul(ph(ob), lhsT=aoall[:, hh, 128 * t:128 * t + 128],
                         rhs=wot_sb[dmg][:, hh, :],
                         start=(hh == 0), stop=(hh == NH - 1))
                for hh in range(NH)][-1],
               reads=[r_ao[t], r_wot[dmg]], writes=[r_pb[ob]])
            em("vector", lambda e, t=t, dmg=dmg, ob=ob: e.tensor_add(
                x1_sb[:, t, 512 * dmg:512 * dmg + 512], ph(ob),
                xbuf2[dmg // 2][:, 512 * (dmg % 2):512 * (dmg % 2) + 512]),
               reads=[r_pb[ob], r_xb2[dmg // 2]], writes=[r_x1[t]],
               alias=([r_kfm, r_vtm, r_qall]
                      if (t == 0 and dmg == 0) else ()))
        if t % 8 < 4:
            em("sync", lambda e, t=t: e.dma_start(
                out=dx1[128 * t:128 * t + 128, :], in_=x1_sb[:, t, :]),
               reads=[r_x1[t]], writes=[r_x1sp[t]], sem="sa")
        if t < 8:
            ln2_stats(0, t)

    for sp in range(NSPAN):
        for tt in range(8):
            ln2_apply(sp, tt)
        def g_dma(i):
            gsl = i % 2
            em("sync", lambda e, i=i, gsl=gsl, sp=sp: e.dma_start(
                out=gf[sp][gsl],
                in_=dgt[i].rearrange("p c h -> p (c h)")),
               writes=[r_gf[gsl]],
               alias=([r_x1[8 * sp + 2 * gsl], r_x1[8 * sp + 2 * gsl + 1]]
                      if i < 2 else ()),
               sem=("ga", "gb")[gsl])

        def g_cast(i):
            gsl = i % 2
            al = ([r_wot[0], r_wot[3], r_pt[0], r_pt[1]] +
                  list(r_ex) + list(r_pr) + [r_dnb[gsl]]
                  if (sp == 0 and i < 2) else
                  ([r_dnb[gsl]] if i < 2 else []))
            em("scalar", lambda e, gsl=gsl, sp=sp: e.copy(
                gub[gsl].rearrange("p c h -> p (c h)"), gf[sp][gsl]),
               reads=[r_gf[gsl]], writes=[r_gub[gsl]], alias=al)

        g_dma(0)
        g_cast(0)
        g_dma(1)
        for i in range(IC):
            gsl = i % 2
            usl = 2 + (i % 2)
            if i + 2 < IC:
                g_dma(i + 2)
            if i + 1 < IC:
                g_cast(i + 1)
            al_g = ([r_wot[0], r_wot[3], r_pt[0], r_pt[1]] +
                    list(r_ex) + list(r_pr)
                    if (sp == 0 and i < 2) else ())
            em("gpsimd", lambda e, i=i, usl=usl: e.dma_start(
                out=gub[usl], in_=dut[i], max_dma_last_dim=8192),
               writes=[r_gub[usl]],
               alias=list(al_g) + [r_osb[usl - 2]], sem=("ua", "ub")[usl - 2])
            for grp in range(2):
                gb = 4 * (i % 2) + grp
                ub = 4 * (i % 2) + 2 + grp
                xres = [r_xn2T[4 * grp + k] for k in range(4)]
                em("tensor", lambda e, grp=grp, gsl=gsl, gb=gb: [
                    e.matmul(ph(gb), lhsT=gub[gsl][:, dc, :],
                             rhs=xn2T[:, dc, 512 * grp:512 * grp + 512],
                             start=(dc == 0), stop=(dc == DC - 1))
                    for dc in range(DC)][-1],
                   reads=xres + [r_gub[gsl]], writes=[r_pb[gb]])
                em("tensor", lambda e, grp=grp, usl=usl, ub=ub: [
                    e.matmul(ph(ub), lhsT=gub[usl][:, dc, :],
                             rhs=xn2T[:, dc, 512 * grp:512 * grp + 512],
                             start=(dc == 0), stop=(dc == DC - 1))
                    for dc in range(DC)][-1],
                   reads=xres + [r_gub[usl]], writes=[r_pb[ub]])
                em("scalar", lambda e, i=i, gb=gb: e.activation(
                    silub, ph(gb), AF.Silu, scale=gs_sb[:, i:i + 1]),
                   reads=[r_pb[gb], r_misc], writes=[r_silu],
                   alias=([r_pt[0], r_pt[1], r_wot[3]]
                          if (sp == 0 and i == 0 and grp == 0) else ()))
                if sp == 0 and i == 0 and grp == 0:
                    hal = ([r_ao[t] for t in range(NT)] +
                           [r_xb2[0], r_xb2[1], r_mk[0], r_mk[1]] +
                           list(r_ex))
                elif sp == 0 and i == 36 and grp == 0:
                    hal = [r_wot[1]]
                else:
                    hal = ()
                em("vector", lambda e, i=i, grp=grp, ub=ub:
                   e.scalar_tensor_tensor(
                       (h_a if i < 36 else h_b)[:, (i if i < 36 else i - 36),
                                                512 * grp:512 * grp + 512],
                       ph(ub), us_sb[:, i:i + 1], silub,
                       ALU.mult, ALU.mult),
                   reads=[r_pb[ub], r_silu, r_misc], writes=[r_h[i]],
                   alias=hal)
        em("sync", lambda e: e.dma_start(out=dsb_sb, in_=ddsb),
           writes=[r_dsb], alias=[r_xn2T[k] for k in range(8)], sem="ds")
        NCH = 4 * 11

        def dn_slot(ci):
            if ci % 2:
                return 1
            return 0 if (ci // 2) % 2 == 0 else 2

        def dn_dma(ci):
            dmg_, i4_ = divmod(ci, 11)
            sl = dn_slot(ci)
            if ci % 2 == 0:
                al = ([r_gub[0]] if ci == 0 else
                      ([r_xnst2] if ci == 2 else ()))
                em("gpsimd", lambda e, dmg_=dmg_, i4_=i4_, sl=sl: e.dma_start(
                    out=dnb[sl], in_=ddnt[dmg_, i4_], max_dma_last_dim=8192),
                   writes=[r_dnb[sl]], alias=al,
                   sem=("ua", "ub")[(ci // 2) % 2])
            else:
                fsl = (ci // 2) % 2
                em("sync", lambda e, dmg_=dmg_, i4_=i4_, fsl=fsl: e.dma_start(
                    out=dnf[fsl],
                    in_=ddnt[dmg_, i4_].rearrange("p k h -> p (k h)")),
                   writes=[r_dnf[fsl]],
                   alias=([r_xn2T[kk] for kk in range(8)] if ci == 1 else ()),
                   sem="ds")

        def dn_cast(ci):
            if ci % 2 == 0:
                return
            fsl = (ci // 2) % 2
            em("scalar", lambda e, fsl=fsl: e.copy(
                dnb[1].rearrange("p k h -> p (k h)"), dnf[fsl]),
               reads=[r_dnf[fsl]], writes=[r_dnb[1]],
               alias=([r_gub[1]] if ci == 1 else ()))

        def x1_reload(d):
            xsl = d % 2
            for tt in range(4):
                em("sync", lambda e, d=d, xsl=xsl, tt=tt, sp=sp: e.dma_start(
                    out=x1r[sp][xsl][:, tt, :],
                    in_=dx1[128 * (8 * sp + tt):128 * (8 * sp + tt) + 128,
                            512 * d:512 * d + 512]),
                   reads=[r_x1sp[8 * sp + tt]], writes=[r_x1r[xsl]],
                   alias=([r_gf[0], r_gf[1]] if tt == 0 else ()),
                   sem=("x2a", "x2b")[xsl])

        dn_dma(0)
        dn_dma(1)
        x1_reload(0)
        for ci in range(NCH):
            dmg, i4 = divmod(ci, 11)
            dsl = dn_slot(ci)
            if ci + 1 < NCH:
                dn_cast(ci + 1)
            if i4 == 5 and dmg + 1 < 4:
                x1_reload(dmg + 1)
            if sp == 0 and 4 <= ci < 36 and ci % 4 == 0:
                ln2_stats(1, (ci - 4) // 4)
            if i4 == 0:
                for tt in range(8):
                    em("tensor", lambda e, tt=tt, dsl=dsl:
                       e.matmul(ph(tt), lhsT=h_ap(0, tt),
                                rhs=dnb[dsl][:, 0, :],
                                start=True, stop=False),
                       reads=[r_h[0], r_dnb[dsl]],
                       writes=[r_pb[tt]])
                em("tensor", lambda e, dsl=dsl: [
                    e.matmul(ph(tt), lhsT=h_ap(k, tt),
                             rhs=dnb[dsl][:, k, :],
                             start=False, stop=False)
                    for k in range(1, 4) for tt in range(8)],
                   reads=[r_h[1], r_h[2], r_h[3], r_dnb[dsl]],
                   writes=[r_pb[tt] for tt in range(8)], n=24)
            else:
                em("tensor", lambda e, i4=i4, dsl=dsl: [
                    e.matmul(ph(tt), lhsT=h_ap(4 * i4 + k, tt),
                             rhs=dnb[dsl][:, k, :],
                             start=False, stop=(4 * i4 + k == IC - 1))
                    for k in range(4) for tt in range(8)],
                   reads=[r_h[4 * i4 + k] for k in range(4)] + [r_dnb[dsl]],
                   writes=[r_pb[tt] for tt in range(8)], n=32)
            if ci + 2 < NCH:
                dn_dma(ci + 2)
            if i4 != 10:
                continue
            for tt in range(8):
                t = 8 * sp + tt
                oslot = tt % 2
                em("vector", lambda e, t=t, tt=tt, dmg=dmg, oslot=oslot:
                   e.tensor_mul(osb[oslot], ph(tt),
                                dsb_sb[:, 512 * dmg:512 * dmg + 512]),
                   reads=[r_pb[tt], r_dsb], writes=[r_osb[oslot]],
                   alias=([r_gub[2 + oslot]]
                          if (sp == 0 and dmg == 0 and tt < 2) else ()))
                if tt < 4:
                    xsrc = x1r[sp][dmg % 2][:, tt, :]
                    xread = r_x1r[dmg % 2]
                else:
                    xsrc = x1_sb[:, t, 512 * dmg:512 * dmg + 512]
                    xread = r_x1[t]
                em("vector", lambda e, oslot=oslot, xsrc=xsrc:
                   e.tensor_add(osb[oslot], osb[oslot], xsrc),
                   reads=[xread], writes=[r_osb[oslot]])
                em("sync", lambda e, t=t, dmg=dmg, oslot=oslot: e.dma_start(
                    out=dout[128 * t:128 * t + 128,
                             512 * dmg:512 * dmg + 512],
                    in_=osb[oslot]),
                   reads=[r_osb[oslot]], sem=("sa", "sb")[oslot])

    if debug:
        dbg_dump("d_x1", x1_sb, [r_x1[t] for t in range(NT)])

    B.barrier_wait("sync", [("sa", B.P.cnt["sa"]), ("sb", B.P.cnt["sb"])])

    import contextlib
    stack = contextlib.ExitStack()
    sem_handles = {s: stack.enter_context(nc.semaphore(s)) for s in SEMS}
    blk = stack.enter_context(nc.Block())

    def runner(queue):
        def run(eng):
            for fn, waits, inc_sem, inc_n in queue:
                for s, v in waits:
                    eng.wait_ge(sem_handles[s], v)
                if fn is None:
                    continue
                inst = fn(eng)
                if isinstance(inst, list):
                    assert len(inst) == inc_n, (len(inst), inc_n)
                    for i_ in inst:
                        i_.then_inc(sem_handles[inc_sem], 1)
                elif inc_sem is not None:
                    inst.then_inc(sem_handles[inc_sem], inc_n)
        return run

    blk.tensor(runner(B.P.q["tensor"]))
    blk.vector(runner(B.P.q["vector"]))
    blk.scalar(runner(B.P.q["scalar"]))
    blk.sync(runner(B.P.q["sync"]))
    blk.gpsimd(runner(B.P.q["gpsimd"]))
    stack.close()
    return nc


_NC_CACHE = None


def kernel(L0_attn_x, L0_attn_wq, L0_attn_wk, L0_attn_wv, L0_attn_wo,
           L0_ffn_gate_w, L0_ffn_up_w, L0_ffn_down_w, L0_ffn_gate_s,
           L0_ffn_up_s, L0_ffn_down_s):
    global _NC_CACHE
    x = np.asarray(L0_attn_x, dtype=np.float32)
    wq = np.asarray(L0_attn_wq, np.float32)
    wk = np.asarray(L0_attn_wk, np.float32)
    wv = np.asarray(L0_attn_wv, np.float32)
    wo = np.asarray(L0_attn_wo, np.float32)
    gw = np.asarray(L0_ffn_gate_w, np.float32)
    uw = np.asarray(L0_ffn_up_w, np.float32)
    dw = np.asarray(L0_ffn_down_w, np.float32)
    gs = np.asarray(L0_ffn_gate_s, np.float32)
    us = np.asarray(L0_ffn_up_s, np.float32)
    ds = np.asarray(L0_ffn_down_s, np.float32)

    shared = {
        "wqt": _prep_lhsT(wq, 16),
        "wkt": _prep_lhsT(wk, 4),
        "wvt": _prep_wvt(wv),
        "wot": _prep_wot(wo),
        "gt": _prep_lhsT(gw, IC),
        "ut": _prep_lhsT(uw, IC),
        "dnt": _prep_dnt(dw),
        "gsc": np.ascontiguousarray(gs.reshape(IC, 128).T),
        "usc": np.ascontiguousarray(us.reshape(IC, 128).T),
        "dsb": np.ascontiguousarray(np.broadcast_to(ds, (128, D))),
        "ide": np.eye(128, dtype=np.float32),
        "cst": np.ascontiguousarray(np.broadcast_to(
            np.array([1.0 / D, EPS, -1.0, 0.0], np.float32), (128, 4))),
    }
    in_maps = []
    for core in range(8):
        s, h = core // 2, core % 2
        m = dict(shared)
        m["xo"] = np.ascontiguousarray(x[s, h * TOK:(h + 1) * TOK])
        m["xh"] = np.ascontiguousarray(
            x[s, 1792:2048] if h == 1 else x[s, 0:HALO])
        m["xs"] = np.ascontiguousarray(x[s, 0:SINK])
        m["mask"] = _make_mask(core)
        in_maps.append(m)

    if _NC_CACHE is None:
        _NC_CACHE = build_nc()
    res = run_bass_kernel_spmd(_NC_CACHE, in_maps, core_ids=list(range(8)))
    out = np.empty((4, 4096, D), dtype=np.float32)
    for core in range(8):
        s, h = core // 2, core % 2
        out[s, h * TOK:(h + 1) * TOK] = res.results[core]["out"]
    return out


# version 34 (final)
# speedup vs baseline: 1.0512x; 1.0065x over previous
import math

import ml_dtypes
import numpy as np

import concourse.bass as bass
import concourse.mybir as mybir
from concourse.bass_utils import run_bass_kernel_spmd

F32 = mybir.dt.float32
BF16 = mybir.dt.bfloat16
AF = mybir.ActivationFunctionType
ALU = mybir.AluOpType
AX = mybir.AxisListType

D = 2048
DC = 16
NH = 16
NKV = 4
HD = 128
INTER = 5632
IC = 44
TOK = 2048
NT = 16
HALO = 256
SINK = 4
BUF_T = HALO + TOK + SINK
NBT = 19
BT_PAD = 2320
NQI = 16
SPAN = 1024
NSPAN = 2
SC_N = 388
NEG = -1.0e30
QSCALE = 1.0 / math.sqrt(HD)
EPS = 1e-5

_DMA_SEMS_L = ["c", "cw", "xa", "xb", "xc", "ma", "mb", "x2a", "x2b", "qa", "qb",
               "wo0", "wo1", "wo2", "wo3", "ga", "gb", "ua", "ub",
               "ds", "sa", "sb"]
SEMS = tuple(["t", "v", "s"] + _DMA_SEMS_L)
DMA_SEMS = tuple(_DMA_SEMS_L)
ENG_SEM = {"tensor": "t", "vector": "v", "scalar": "s"}
ENGS = ("tensor", "vector", "scalar", "sync", "gpsimd")


def _make_mask(core):
    h = core % 2
    base = h * TOK
    qg = base + (128 * np.arange(NQI))[:, None, None] + \
        np.arange(128)[None, :, None]
    kb = 128 * np.arange(NQI)[:, None, None] + np.arange(384)[None, None, :]
    if h == 1:
        kg = np.where(kb < HALO, 1792 + kb, base + (kb - HALO))
        valid = np.ones(np.broadcast_shapes(qg.shape, kb.shape), dtype=bool)
    else:
        kg = np.where(kb < HALO, 0, base + (kb - HALO))
        valid = np.broadcast_to(kb >= HALO,
                                np.broadcast_shapes(qg.shape, kb.shape))
    ok = valid & (kg <= qg) & ((kg < SINK) | (kg >= qg - 255))
    m = np.full((NQI, 128, SC_N), NEG, dtype=np.float32)
    m[:, :, :384] = np.where(ok, 0.0, NEG)
    for qi in range(NQI):
        for j in range(SINK):
            if h == 0 and (128 * qi) <= (HALO + j) < (128 * qi + 384):
                continue
            qcol = base + 128 * qi + np.arange(128)
            m[qi, :, 384 + j] = np.where(j <= qcol, 0.0, NEG)
    return m


def _prep_lhsT(w, oc):
    return np.ascontiguousarray(w.reshape(oc, 128, DC, 128).transpose(0, 3, 2, 1))


def _prep_wvt(wv):
    return np.ascontiguousarray(
        np.ascontiguousarray(wv.T).reshape(DC, 128, 512).transpose(1, 0, 2))


def _prep_wot(wo):
    return np.ascontiguousarray(wo.reshape(4, 512, 16, 128).transpose(0, 3, 2, 1))


def _prep_dnt(dw):
    t = np.ascontiguousarray(dw.T).reshape(11, 4, 128, 4, 512)
    return np.ascontiguousarray(t.transpose(3, 0, 2, 1, 4))


class Prog:
    def __init__(self):
        self.q = {e: [] for e in ENGS}
        self.cnt = {s: 0 for s in SEMS}

    def op(self, eng, fn, waits, inc_sem, inc_n):
        w = {}
        own = ENG_SEM.get(eng)
        for tok in waits:
            if tok is None:
                continue
            s, v = tok
            if s in DMA_SEMS:
                v = self.cnt[s]
            if w.get(s, 0) < v:
                w[s] = v
        tok = None
        if inc_sem is not None:
            if inc_sem in DMA_SEMS and self.cnt[inc_sem] > 0:
                if w.get(inc_sem, 0) < self.cnt[inc_sem]:
                    w[inc_sem] = self.cnt[inc_sem]
            self.cnt[inc_sem] += inc_n
            tok = (inc_sem, self.cnt[inc_sem])
        self.q[eng].append((fn, sorted(w.items()), inc_sem, inc_n))
        return tok


class Res:
    __slots__ = ("w", "rs")

    def __init__(self):
        self.w = None
        self.rs = []


class Build:
    def __init__(self):
        self.P = Prog()

    def emit(self, eng, fn, reads=(), writes=(), alias=(), sem=None, n=None):
        waits = []
        for r in reads:
            waits.append(r.w)
        for r in writes:
            waits.append(r.w)
            waits.extend(r.rs)
        for r in alias:
            waits.append(r.w)
            waits.extend(r.rs)
        inc = sem if sem is not None else ENG_SEM[eng]
        inc_n = n if n is not None else (16 if inc in DMA_SEMS else 1)
        t = self.P.op(eng, fn, waits, inc, inc_n)
        for r in reads:
            r.rs.append(t)
        for r in writes:
            r.w = t
            r.rs = []
        return t

    def barrier_wait(self, eng, tokens):
        self.P.op(eng, None, [t for t in tokens if t], None, 0)


def build_nc(debug=False):
    DEBUG = [debug]
    nc = bass.Bass("TRN2", target_bir_lowering=False)
    B = Build()
    em = B.emit

    dxo = nc.dram_tensor("xo", [TOK, D], F32, kind="ExternalInput").ap()
    dxh = nc.dram_tensor("xh", [HALO, D], F32, kind="ExternalInput").ap()
    dxs = nc.dram_tensor("xs", [SINK, D], F32, kind="ExternalInput").ap()
    dmask = nc.dram_tensor("mask", [NQI, 128, SC_N], F32, kind="ExternalInput").ap()
    dwqt = nc.dram_tensor("wqt", [16, 128, DC, 128], F32, kind="ExternalInput").ap()
    dwkt = nc.dram_tensor("wkt", [4, 128, DC, 128], F32, kind="ExternalInput").ap()
    dwvt = nc.dram_tensor("wvt", [128, DC, 512], F32, kind="ExternalInput").ap()
    dwot = nc.dram_tensor("wot", [4, 128, 16, 512], F32, kind="ExternalInput").ap()
    dgt = nc.dram_tensor("gt", [IC, 128, DC, 128], F32, kind="ExternalInput").ap()
    dut = nc.dram_tensor("ut", [IC, 128, DC, 128], F32, kind="ExternalInput").ap()
    ddnt = nc.dram_tensor("dnt", [4, 11, 128, 4, 512], F32, kind="ExternalInput").ap()
    dgs = nc.dram_tensor("gsc", [128, IC], F32, kind="ExternalInput").ap()
    dus = nc.dram_tensor("usc", [128, IC], F32, kind="ExternalInput").ap()
    ddsb = nc.dram_tensor("dsb", [128, D], F32, kind="ExternalInput").ap()
    dide = nc.dram_tensor("ide", [128, 128], F32, kind="ExternalInput").ap()
    dcst = nc.dram_tensor("cst", [128, 4], F32, kind="ExternalInput").ap()
    dout = nc.dram_tensor("out", [TOK, D], F32, kind="ExternalOutput").ap()
    dx1 = nc.dram_tensor("x1spill", [TOK, D], BF16, kind="Internal").ap()

    SB_BASE = 16512

    def sb(name, off, shape, dtype):
        import math as _m
        nb = off + SB_BASE + int(np.prod(shape[1:])) * (
            4 if dtype == F32 else 2)
        assert nb <= 229376, (name, nb)
        return nc.alloc_sbuf_tensor_at(name, shape, dtype,
                                       offset=off + SB_BASE).ap()

    idf = sb("idf", 0, [128, 128], F32)
    stats = sb("stats", 512, [128, 24], F32)
    gs_sb = sb("gs_sb", 608, [128, IC], F32)
    us_sb = sb("us_sb", 800, [128, IC], F32)
    cst_sb = sb("cst_sb", 992, [128, 4], F32)

    xnT = sb("xnT", 1024, [128, DC, BT_PAD], BF16)
    aoall = sb("aoall", 1024, [128, NH, TOK], BF16)
    xbuf2 = [sb(f"xb2_{i}", 66560 + 4096 * i, [128, 1024], F32)
             for i in range(2)]
    h_a = sb("h_a", 1024, [128, 36, SPAN], BF16)

    kfm = sb("kfm", 75264, [128, NKV, BT_PAD], BF16)
    vtm = sb("vtm", 93824, [128, NBT, NKV, HD], BF16)
    x1_sb = sb("x1", 75264, [128, NT, D], BF16)

    qall = sb("qall", 113280, [128, NH, TOK], BF16)
    xbuf = [sb(f"xb_{i}", off, [128, D], F32)
            for i, off in enumerate((113280, 121472, 178816))]
    xnst = [sb(f"xnst_{i}", 129664 + 8192 * i, [128, D], F32)
            for i in range(2)]
    wkt_sb = sb("wkt_sb", 146048, [128, 4, DC, 128], BF16)
    wvt_sb = sb("wvt_sb", 162432, [128, DC, 512], BF16)

    wqb = [sb(f"wqb_{i}", 178816 + 4096 * i, [128, DC, 128], BF16)
           for i in range(2)]
    wot_sb = [
        sb("wot0", 178816, [128, 16, 512], BF16),
        sb("wot1", 140800, [128, 16, 512], BF16),
        sb("wot2", 157184, [128, 16, 512], BF16),
        sb("wot3", 195200, [128, 16, 512], BF16),
    ]

    exb = [sb(f"ex_{i}", 69696 + 800 * i, [128, 392], BF16)
           for i in range(4)]
    prb = [sb(f"pr_{i}", 198336 + 1568 * i, [128, 392], F32)
           for i in range(4)]
    ptb = [sb(f"pt_{i}", 204608 + 4096 * i, [128, 4, 4, 128], BF16)
           for i in range(2)]
    mkb = [sb(f"mk_{i}", 66560 + 1568 * i, [128, SC_N], F32)
           for i in range(2)]

    h_b = sb("h_b", 140800, [128, 8, SPAN], BF16)
    xn2T = sb("xn2T", 157184, [128, DC, SPAN], BF16)
    dsb_sb = sb("dsb_sb", 157184, [128, D], F32)
    gub = [sb(f"gub_{i}", 189952 + 4096 * i, [128, DC, 128], BF16)
           for i in range(4)]
    dnb = [sb(f"dnb_{i}", 189952 + 4096 * i, [128, 4, 512], BF16)
           for i in range(2)]
    dnb.append(sb("dnb_2", 206336, [128, 4, 512], BF16))
    dnf = [sb(f"dnf_{i}", off, [128, D], F32)
           for i, off in enumerate((165376, 177664))]
    gf = [[sb(f"gf_{s}{j}", 75264 + 32768 * s + 8192 * j, [128, D], F32)
           for j in range(2)] for s in range(2)]
    x1r = [[sb(f"x1r_{s}{j}", 75264 + 32768 * s + 4096 * j,
               [128, 4, 512], BF16) for j in range(2)]
           for s in range(2)]
    osb = [sb(f"osb_{i}", 198144 + 4096 * i, [128, 512], F32)
           for i in range(2)]
    xnst2 = sb("xnst2", 206336, [128, 1024], F32)
    lnjunk = sb("lnjunk", 173568, [128, 1024], F32)
    ajunk = sb("ajunk", 187008, [128, D], F32)
    silub = sb("silub", 210432, [128, 512], F32)

    pbt = [nc.alloc_psum_tensor(f"pb{i}", [128, 1024], F32).ap()
           for i in range(4)]

    def ph(i):
        return pbt[i // 2][:, 512 * (i % 2):512 * (i % 2) + 512]

    def dbg_dump(nm, ap_, rr):
        dd = nc.dram_tensor(nm, list(ap_.shape), ap_.dtype,
                            kind="ExternalOutput").ap()
        em("sync", lambda e, dd=dd, ap_=ap_: e.dma_start(out=dd, in_=ap_),
           reads=list(rr), sem="sa")

    r_pb = [Res() for _ in range(8)]
    r_xnT = [Res() for _ in range(NBT)]
    r_kfm, r_vtm, r_qall = Res(), Res(), Res()
    r_wkt, r_wvt = Res(), Res()
    r_wqb = [Res(), Res()]
    r_wot = [Res() for _ in range(4)]
    r_xbuf = [Res(), Res(), Res()]
    r_xnst = [Res(), Res()]
    r_xb2 = [Res(), Res()]
    r_ex = [Res() for _ in range(4)]
    r_pr = [Res() for _ in range(4)]
    r_pt = [Res(), Res()]
    r_mk = [Res(), Res()]
    r_ao = [Res() for _ in range(NT)]
    r_x1 = [Res() for _ in range(NT)]
    r_h = [Res() for _ in range(IC)]
    r_xn2T = [Res() for _ in range(8)]
    r_gub = [Res() for _ in range(4)]
    r_dnb = [Res(), Res(), Res()]
    r_dnf = [Res(), Res()]
    r_gf = [Res(), Res()]
    r_x1r = [Res(), Res()]
    r_x1sp = [Res() for _ in range(NT)]
    r_osb = [Res(), Res()]
    r_stats = [Res(), Res()]
    r_bst = [Res() for _ in range(4)]
    r_xnst2 = Res()
    r_lnj = Res()
    r_ajk = Res()
    r_red = [Res(), Res()]
    r_silu = Res()
    r_dsb = Res()
    r_misc = Res()

    nc.const_aps.register(F32, 1.0 / D, cst_sb[:, 0:1])
    nc.const_aps.register(F32, EPS, cst_sb[:, 1:2])
    nc.const_aps.register(F32, -1.0, cst_sb[:, 2:3])

    em("sync", lambda e: e.dma_start(out=cst_sb, in_=dcst), writes=[r_misc], sem="c")
    em("sync", lambda e: e.dma_start(out=idf, in_=dide), writes=[r_misc], sem="c")
    em("sync", lambda e: e.dma_start(out=gs_sb, in_=dgs), writes=[r_misc], sem="c")
    em("sync", lambda e: e.dma_start(out=us_sb, in_=dus), writes=[r_misc], sem="c")

    em("gpsimd", lambda e: e.dma_start(
        out=wkt_sb, in_=dwkt.rearrange("h p c i -> p h c i"),
        max_dma_last_dim=8192), writes=[r_wkt], sem="cw")
    em("gpsimd", lambda e: e.dma_start(
        out=wvt_sb, in_=dwvt, max_dma_last_dim=8192), writes=[r_wvt], sem="cw")

    def ln_front(src_ap, rows, read_res, sset, alias1=()):
        st = stats[0:rows, 8 * sset:8 * sset + 8]
        s_nm, s_sq = st[:, 0:1], st[:, 1:2]
        rs = [read_res] if read_res is not None else []
        em("vector", lambda e: e.reduce_sum(s_nm, src_ap, AX.X, negate=True),
           reads=rs, writes=[r_red[sset]])
        em("scalar", lambda e: e.activation(ajunk[0:rows, :], src_ap,
                                            AF.Square, accum_out=s_sq),
           reads=rs, writes=[r_stats[sset], r_ajk], alias=alias1)

    def ln_back(src_ap, rows, xout, xout_res, read_res, sset):
        st = stats[0:rows, 8 * sset:8 * sset + 8]
        s_nm, s_sq, s_mn, s_m2, s_ve, s_var, s_sd, s_rs = (
            st[:, i:i + 1] for i in range(8))
        s_nb = s_var
        rs = [read_res] if read_res is not None else []
        em("vector", lambda e: e.tensor_scalar_mul(s_mn, s_nm, 1.0 / D),
           reads=[r_red[sset]], writes=[r_stats[sset]])
        em("vector", lambda e: e.tensor_scalar(s_ve, s_sq, 1.0 / D, EPS,
                                               ALU.mult, ALU.add),
           writes=[r_stats[sset]])
        em("vector", lambda e: e.tensor_mul(s_m2, s_mn, s_mn),
           writes=[r_stats[sset]])
        em("vector", lambda e: e.tensor_sub(s_var, s_ve, s_m2),
           writes=[r_stats[sset]])
        em("scalar", lambda e: e.activation(s_sd, s_var, AF.Sqrt),
           writes=[r_stats[sset]])
        em("vector", lambda e: e.reciprocal(s_rs, s_sd),
           writes=[r_stats[sset]])
        em("vector", lambda e: e.tensor_mul(s_nb, s_mn, s_rs),
           writes=[r_stats[sset]])
        em("vector", lambda e: e.tensor_scalar(xout, src_ap, s_rs, s_nb,
                                               ALU.mult, ALU.add),
           reads=rs + [r_stats[sset]], writes=[xout_res])

    def transpose_tile(xn_ap, xn_res, rows, dst, dst_res, dst_col,
                       extra_alias=()):
        for hf in range(4):
            bank = 4 + hf
            em("tensor", lambda e, hf=hf, rw=rows: [
                e.transpose(out=ph(4 + hf)[:, 128 * c:128 * c + rw],
                            in_=xn_ap[:, 128 * (4 * hf + c):
                                      128 * (4 * hf + c) + 128],
                            identity=idf[0:rw, 0:rw])
                for c in range(4)],
               reads=[xn_res, r_misc], writes=[r_pb[bank]], n=4)
            evac_eng = "scalar" if hf % 2 == 0 else "vector"

            def _evac(e, hf=hf, rw=rows):
                view = ph(4 + hf).rearrange("p (c q) -> p c q", c=4)[:, :, 0:rw]
                dstv = dst[:, 4 * hf:4 * hf + 4, dst_col:dst_col + rw]
                return (e.copy(dstv, view) if hf % 2 == 0
                        else e.tensor_copy(dstv, view))
            em(evac_eng, _evac,
               reads=[r_pb[bank]], writes=[dst_res], alias=extra_alias)

    def a_src(b):
        if b < 2:
            return dxh[128 * b:128 * b + 128]
        if b < 18:
            return dxo[128 * (b - 2):128 * (b - 2) + 128]
        return dxs

    def a_tr_stage(b):
        rows = 4 if b == NBT - 1 else 128
        sset = b % 2
        transpose_tile(xnst[sset][0:rows, :], r_xnst[sset], rows,
                       xnT, r_xnT[b], 128 * b)

    def a_proj_stage(b):
        rows = 4 if b == NBT - 1 else 128
        vb = b % 2
        em("tensor", lambda e, b=b, rw=rows, vb=vb: [
            e.matmul(ph(vb)[0:rw, :],
                     lhsT=xnT[:, dc, 128 * b:128 * b + rw],
                     rhs=wvt_sb[:, dc, :],
                     start=(dc == 0), stop=(dc == DC - 1))
            for dc in range(DC)][-1],
           reads=[r_xnT[b], r_wvt], writes=[r_pb[vb]])
        em("scalar", lambda e, b=b, rw=rows, vb=vb: e.copy(
            vtm[0:rw, b, :, :],
            ph(vb)[0:rw, :].rearrange("p (k h) -> p k h", k=NKV)),
           reads=[r_pb[vb]], writes=[r_vtm])
        if b % 4 == 3 or b == NBT - 1:
            g = b // 4
            g0 = 512 * g
            ng = min(512, BUF_T - g0)
            tiles = [r_xnT[tt] for tt in range(4 * g, min(4 * g + 4, NBT))]
            for hc in range(4):
                kb = 2 + (hc % 2)
                em("tensor", lambda e, hc=hc, g0=g0, ng=ng, kb=kb: [
                    e.matmul(ph(kb)[:, 0:ng],
                             lhsT=wkt_sb[:, hc, dc, :],
                             rhs=xnT[:, dc, g0:g0 + ng],
                             start=(dc == 0), stop=(dc == DC - 1))
                    for dc in range(DC)][-1],
                   reads=tiles + [r_wkt], writes=[r_pb[kb]])
                em("scalar", lambda e, hc=hc, g0=g0, ng=ng, kb=kb: e.copy(
                    kfm[:, hc, g0:g0 + ng], ph(kb)[:, 0:ng]),
                   reads=[r_pb[kb]], writes=[r_kfm])

    for b in range(NBT + 3):
        if b < NBT:
            rows = 4 if b == NBT - 1 else 128
            sl = b % 3
            em("sync", lambda e, s=a_src(b), sl=sl, rw=rows: e.dma_start(
                out=xbuf[sl][0:rw, :], in_=s), writes=[r_xbuf[sl]],
               sem=("xa", "xb", "xc")[sl])
            ln_front(xbuf[sl][0:rows, :], rows, r_xbuf[sl], b % 2)
        if 1 <= b <= NBT:
            bb = b - 1
            rows = 4 if bb == NBT - 1 else 128
            sl = bb % 3
            sset = bb % 2
            ln_back(xbuf[sl][0:rows, :], rows, xnst[sset][0:rows, :],
                    r_xnst[sset], r_xbuf[sl], sset)
        if 2 <= b <= NBT + 1:
            a_tr_stage(b - 2)
        if 3 <= b:
            a_proj_stage(b - 3)

    for hc in range(16):
        sl = hc % 2
        em("gpsimd", lambda e, hc=hc, sl=sl: e.dma_start(
            out=wqb[sl], in_=dwqt[hc], max_dma_last_dim=8192),
           writes=[r_wqb[sl]],
           alias=([r_xbuf[2]] if hc < 2 else ()),
           sem=("qa", "qb")[sl])
        for g in range(4):
            qb = 4 + ((hc * 4 + g) % 2)
            tiles = [r_xnT[2 + 4 * g + i] for i in range(4)]
            em("tensor", lambda e, hc=hc, g=g, sl=sl, qb=qb: [
                e.matmul(ph(qb),
                         lhsT=wqb[sl][:, dc, :],
                         rhs=xnT[:, dc, HALO + 512 * g:HALO + 512 * g + 512],
                         start=(dc == 0), stop=(dc == DC - 1))
                for dc in range(DC)][-1],
               reads=tiles + [r_wqb[sl]], writes=[r_pb[qb]])
            em("vector", lambda e, hc=hc, g=g, qb=qb: e.tensor_scalar_mul(
                qall[:, hc, 512 * g:512 * g + 512], ph(qb), QSCALE),
               reads=[r_pb[qb]], writes=[r_qall],
               alias=([r_xbuf[0], r_xbuf[1], r_xnst[0], r_xnst[1],
                       r_wkt, r_wvt] if (hc == 0 and g == 0) else ()))

    if debug:
        dbg_dump("d_xnT", xnT, r_xnT)

    em("gpsimd", lambda e: e.dma_start(out=wot_sb[0], in_=dwot[0],
                                       max_dma_last_dim=8192),
       writes=[r_wot[0]], alias=[r_wqb[0], r_wqb[1], r_ajk], sem="wo0")

    st_neg = [stats[:, 8 + i:9 + i] for i in range(4)]
    st_sum = [stats[:, 12 + i:13 + i] for i in range(4)]
    st_den = [stats[:, 16 + i:17 + i] for i in range(4)]
    st_rec = [stats[:, 20 + i:21 + i] for i in range(4)]

    def sm_scores(qi, kv, j):
        head = 4 * kv + j
        sbk = j
        em("tensor", lambda e, qi=qi, head=head, sbk=sbk: [
            e.matmul(ph(sbk)[:, 0:384],
                     lhsT=qall[:, head, 128 * qi:128 * qi + 128],
                     rhs=kfm[:, head // 4, 128 * qi:128 * qi + 384],
                     start=True, stop=True),
            e.matmul(ph(sbk)[:, 384:SC_N],
                     lhsT=qall[:, head, 128 * qi:128 * qi + 128],
                     rhs=kfm[:, head // 4, 2304:2308],
                     start=True, stop=True)],
           reads=[r_qall, r_kfm], writes=[r_pb[sbk]], n=2)

    def sm_stats(qi, kv, j):
        sbk = j
        al0 = ([r_stats[0], r_stats[1], r_red[0], r_red[1]]
               if (qi == 0 and kv == 0 and j == 0) else ())
        em("vector", lambda e, qi=qi, sbk=sbk: e.tensor_add(
            ph(sbk)[:, 0:SC_N], ph(sbk)[:, 0:SC_N], mkb[qi % 2]),
           reads=[r_mk[qi % 2]], writes=[r_pb[sbk]])
        em("vector", lambda e, sbk=sbk, j=j: e.reduce_max(
            st_neg[j], ph(sbk)[:, 0:SC_N], AX.X, negate=True),
           reads=[r_pb[sbk]], writes=[r_bst[j]], alias=al0)

    def sm_exp(qi, kv, j):
        sbk = j
        em("scalar", lambda e, sbk=sbk, j=j: e.activation(
            exb[j][:, 0:SC_N], ph(sbk)[:, 0:SC_N], AF.Exp,
            bias=st_neg[j], accum_out=st_sum[j]),
           reads=[r_pb[sbk], r_bst[j]], writes=[r_ex[j], r_bst[j]],
           alias=([r_xnT[tt] for tt in range(NBT)]
                  if (qi == 0 and kv == 0) else ()))

    def sm_norm(qi, kv, j):
        em("vector", lambda e, j=j: e.tensor_scalar_max(
            st_den[j], st_sum[j], 1e-9), writes=[r_bst[j]])
        em("vector", lambda e, j=j: e.reciprocal(
            st_rec[j], st_den[j]), writes=[r_bst[j]])
        em("vector", lambda e, j=j: e.tensor_scalar_mul(
            prb[j][:, 0:SC_N], exb[j][:, 0:SC_N], st_rec[j]),
           reads=[r_ex[j], r_bst[j]], writes=[r_pr[j]])

    def transpose_head(qi, kv, j, score=None):
        it = qi * 4 + kv
        psl = it % 2
        pslot = j
        tb = 6 + (j % 2)
        tr_ops = lambda e, pslot=pslot, tb=tb: [
            e.transpose(out=ph(tb)[:, 128 * s:128 * s + 128],
                        in_=prb[pslot][:, 128 * s:128 * s + 128],
                        identity=idf)
            for s in range(3)] + [
            e.transpose(out=ph(tb)[0:4, 384:512],
                        in_=prb[pslot][:, 384:SC_N],
                        identity=idf)]
        if score is None:
            em("tensor", tr_ops, reads=[r_pr[pslot], r_misc],
               writes=[r_pb[tb]], n=4)
        else:
            sqi, skv = score
            head = 4 * skv + j
            sbk = j
            em("tensor", lambda e, sqi=sqi, head=head, sbk=sbk: tr_ops(e) + [
                e.matmul(ph(sbk)[:, 0:384],
                         lhsT=qall[:, head, 128 * sqi:128 * sqi + 128],
                         rhs=kfm[:, head // 4, 128 * sqi:128 * sqi + 384],
                         start=True, stop=True),
                e.matmul(ph(sbk)[:, 384:SC_N],
                         lhsT=qall[:, head, 128 * sqi:128 * sqi + 128],
                         rhs=kfm[:, head // 4, 2304:2308],
                         start=True, stop=True)],
               reads=[r_pr[pslot], r_misc, r_qall, r_kfm],
               writes=[r_pb[tb], r_pb[sbk]], n=6)
        em("scalar", lambda e, j=j, tb=tb, psl=psl: [
            e.copy(ptb[psl][:, 0:3, j, :],
                   ph(tb)[:, 0:384].rearrange("p (s q) -> p s q", s=3)),
            e.copy(ptb[psl][0:4, 3, j, :],
                   ph(tb)[0:4, 384:512])],
           reads=[r_pb[tb]], writes=[r_pt[psl]], n=2)

    def pv_evac(qi, kv):
        it = qi * 4 + kv
        psl = it % 2
        ab = 4 + (it % 2)
        em("tensor", lambda e, qi=qi, kv=kv, psl=psl, ab=ab: [
            e.matmul(ph(ab), lhsT=vtm[:, qi + s, kv, :],
                     rhs=ptb[psl][:, s, :, :], start=(s == 0), stop=False)
            for s in range(3)] + [
            e.matmul(ph(ab), lhsT=vtm[0:4, 18, kv, :],
                     rhs=ptb[psl][0:4, 3, :, :], start=False, stop=True)],
           reads=[r_pt[psl], r_vtm], writes=[r_pb[ab]], n=4)
        if DEBUG[0] and qi == 4 and kv == 0:
            dbg_dump("d_pt", ptb[psl], [r_pt[psl]])
        em("scalar", lambda e, qi=qi, kv=kv, ab=ab: e.copy(
            aoall[:, 4 * kv:4 * kv + 4, 128 * qi:128 * qi + 128],
            ph(ab).rearrange("p (j q) -> p j q", j=4)),
           reads=[r_pb[ab]], writes=[r_ao[qi]],
           alias=([r_xnT[t] for t in range(NBT)]
                  if (qi == 0 and kv == 0) else ()))

    prev = None
    for qi in range(NQI):
        em("sync", lambda e, qi=qi: e.dma_start(out=mkb[qi % 2],
                                                in_=dmask[qi]),
           writes=[r_mk[qi % 2]],
           alias=([r_xnT[tt] for tt in range(NBT)] if qi < 2 else ()),
           sem=("ma", "mb")[qi % 2])
        for kv in range(NKV):
            for j in range(4):
                if prev is not None:
                    transpose_head(prev[0], prev[1], j, score=(qi, kv))
                else:
                    sm_scores(qi, kv, j)
            for j in range(4):
                sm_stats(qi, kv, j)
            for j in range(4):
                sm_exp(qi, kv, j)
            for j in range(4):
                sm_norm(qi, kv, j)
            if prev is not None:
                pv_evac(*prev)
            prev = (qi, kv)
    for j in range(4):
        transpose_head(prev[0], prev[1], j)
    pv_evac(*prev)

    for g in range(1, 4):
        al = [r_qall] if g in (1, 2) else \
            list(r_ex) + list(r_pr) + [r_pt[0], r_pt[1]]
        em("gpsimd", lambda e, g=g: e.dma_start(
            out=wot_sb[g], in_=dwot[g], max_dma_last_dim=8192),
           writes=[r_wot[g]], alias=al, sem=f"wo{g}")

    if debug:
        dbg_dump("d_kfm", kfm, [r_kfm])
        dbg_dump("d_vtm", vtm, [r_vtm])
        dbg_dump("d_qall", qall, [r_qall])
        dbg_dump("d_ao", aoall, [r_ao[t] for t in range(NT)])

    def h_ap(i, tt):
        buf, ii = (h_a, i) if i < 36 else (h_b, i - 36)
        return buf[:, ii, 128 * tt:128 * tt + 128]

    def ln2_stats(sp, tt):
        t = 8 * sp + tt
        w = stats[:, 16:24]
        w_nm, w_sq, w_sq2, w_sqt, w_mn, w_m2, w_sd, w_ve = (
            w[:, i:i + 1] for i in range(8))
        w_var = w_sq
        f_rs = stats[:, 2 * tt:2 * tt + 1]
        f_nb = stats[:, 2 * tt + 1:2 * tt + 2]
        al1 = (list(r_bst) if (sp == 0 and tt == 0) else ())
        alj = ([r_qall] if (sp == 0 and tt == 0) else
               ([r_xn2T[k] for k in range(8)] if (sp == 1 and tt == 0)
                else ()))
        em("vector", lambda e: e.reduce_sum(w_nm, x1_sb[:, t, :], AX.X,
                                            negate=True),
           reads=[r_x1[t]], writes=[r_bst[0]], alias=al1)
        em("scalar", lambda e: e.activation(
            lnjunk, x1_sb[:, t, 0:1024], AF.Square, accum_out=w_sq),
           reads=[r_x1[t]], writes=[r_bst[0], r_lnj], alias=alj)
        em("scalar", lambda e: e.activation(
            lnjunk, x1_sb[:, t, 1024:2048], AF.Square, accum_out=w_sq2),
           reads=[r_x1[t]], writes=[r_bst[0], r_lnj])
        em("scalar", lambda e: e.activation(w_mn, w_nm, AF.Identity,
                                            scale=1.0 / D),
           reads=[r_misc], writes=[r_bst[0]])
        em("scalar", lambda e: e.activation(w_sqt, w_sq, AF.Identity,
                                            bias=w_sq2),
           writes=[r_bst[0]])
        em("scalar", lambda e: e.activation(w_m2, w_mn, AF.Square),
           writes=[r_bst[0]])
        em("scalar", lambda e: e.activation(w_ve, w_sqt, AF.Identity,
                                            scale=1.0 / D, bias=EPS),
           writes=[r_bst[0]])
        em("scalar", lambda e: e.activation(w_var, w_m2, AF.Identity,
                                            scale=-1.0, bias=w_ve),
           writes=[r_bst[0]])
        em("scalar", lambda e: e.activation(w_sd, w_var, AF.Sqrt),
           writes=[r_bst[0]])
        em("vector", lambda e: e.reciprocal(f_rs, w_sd),
           reads=[r_bst[0]], writes=[r_stats[0], r_stats[1]])
        em("scalar", lambda e: e.activation(f_nb, w_mn, AF.Identity,
                                            scale=f_rs),
           reads=[r_bst[0]], writes=[r_stats[0], r_stats[1]])

    def ln2_apply(sp, tt):
        t = 8 * sp + tt
        f_rs = stats[:, 2 * tt:2 * tt + 1]
        f_nb = stats[:, 2 * tt + 1:2 * tt + 2]
        for hh in range(2):
            em("scalar", lambda e, hh=hh: e.activation(
                xnst2, x1_sb[:, t, 1024 * hh:1024 * hh + 1024],
                AF.Identity, bias=f_nb, scale=f_rs),
               reads=[r_x1[t], r_stats[0], r_stats[1]], writes=[r_xnst2],
               alias=([r_pt[0], r_pt[1], r_wot[3]]
                      if (sp == 0 and tt == 0 and hh == 0) else
                      ([r_dnb[2]] if (tt == 0 and hh == 0) else ())))
            for sub in range(2):
                bank = 4 + 2 * hh + sub
                em("tensor", lambda e, sub=sub, bank=bank: [
                    e.transpose(out=ph(bank)[:, 128 * c:128 * c + 128],
                                in_=xnst2[:, 128 * (4 * sub + c):
                                          128 * (4 * sub + c) + 128],
                                identity=idf)
                    for c in range(4)],
                   reads=[r_xnst2, r_misc], writes=[r_pb[bank]], n=4)
                dc0 = 8 * hh + 4 * sub
                em("vector", lambda e, dc0=dc0, bank=bank, tt=tt:
                   e.tensor_copy(
                       xn2T[:, dc0:dc0 + 4, 128 * tt:128 * tt + 128],
                       ph(bank).rearrange("p (c q) -> p c q", c=4)),
                   reads=[r_pb[bank]], writes=[r_xn2T[tt]],
                   alias=([r_wot[2], r_wot[0], r_qall, r_dsb, r_lnj,
                           r_dnf[0], r_dnf[1]]
                          if (tt == 0 and hh == 0 and sub == 0) else ()))

    for t in range(NT):
        for half in range(2):
            em("sync", lambda e, t=t, half=half: e.dma_start(
                out=xbuf2[half],
                in_=dxo[128 * t:128 * t + 128,
                        1024 * half:1024 * half + 1024]),
               writes=[r_xb2[half]],
               alias=(([r_xnT[tt] for tt in range(NBT)] +
                       [r_mk[0], r_mk[1]] + list(r_ex)) if t == 0 else ()),
               sem=("x2a", "x2b")[half])
        for dmg in range(4):
            ob = dmg if t % 2 == 0 else 4 + dmg
            em("tensor", lambda e, t=t, dmg=dmg, ob=ob: [
                e.matmul(ph(ob), lhsT=aoall[:, hh, 128 * t:128 * t + 128],
                         rhs=wot_sb[dmg][:, hh, :],
                         start=(hh == 0), stop=(hh == NH - 1))
                for hh in range(NH)][-1],
               reads=[r_ao[t], r_wot[dmg]], writes=[r_pb[ob]])
            em("vector", lambda e, t=t, dmg=dmg, ob=ob: e.tensor_add(
                x1_sb[:, t, 512 * dmg:512 * dmg + 512], ph(ob),
                xbuf2[dmg // 2][:, 512 * (dmg % 2):512 * (dmg % 2) + 512]),
               reads=[r_pb[ob], r_xb2[dmg // 2]], writes=[r_x1[t]],
               alias=([r_kfm, r_vtm, r_qall]
                      if (t == 0 and dmg == 0) else ()))
        if t % 8 < 4:
            em("sync", lambda e, t=t: e.dma_start(
                out=dx1[128 * t:128 * t + 128, :], in_=x1_sb[:, t, :]),
               reads=[r_x1[t]], writes=[r_x1sp[t]], sem="sa")
        if t < 8:
            ln2_stats(0, t)

    for sp in range(NSPAN):
        for tt in range(8):
            ln2_apply(sp, tt)
        def g_dma(i):
            gsl = i % 2
            em("sync", lambda e, i=i, gsl=gsl, sp=sp: e.dma_start(
                out=gf[sp][gsl],
                in_=dgt[i].rearrange("p c h -> p (c h)")),
               writes=[r_gf[gsl]],
               alias=([r_x1[8 * sp + 2 * gsl], r_x1[8 * sp + 2 * gsl + 1]]
                      if i < 2 else ()),
               sem=("ga", "gb")[gsl])

        def g_cast(i):
            gsl = i % 2
            al = ([r_wot[0], r_wot[3], r_pt[0], r_pt[1]] +
                  list(r_ex) + list(r_pr) + [r_dnb[gsl]]
                  if (sp == 0 and i < 2) else
                  ([r_dnb[gsl]] if i < 2 else []))
            em("scalar", lambda e, gsl=gsl, sp=sp: e.copy(
                gub[gsl].rearrange("p c h -> p (c h)"), gf[sp][gsl]),
               reads=[r_gf[gsl]], writes=[r_gub[gsl]], alias=al)

        g_dma(0)
        g_cast(0)
        g_dma(1)
        for i in range(IC):
            gsl = i % 2
            usl = 2 + (i % 2)
            if i + 2 < IC:
                g_dma(i + 2)
            if i + 1 < IC:
                g_cast(i + 1)
            al_g = ([r_wot[0], r_wot[3], r_pt[0], r_pt[1]] +
                    list(r_ex) + list(r_pr)
                    if (sp == 0 and i < 2) else ())
            em("gpsimd", lambda e, i=i, usl=usl: e.dma_start(
                out=gub[usl], in_=dut[i], max_dma_last_dim=8192),
               writes=[r_gub[usl]],
               alias=list(al_g) + [r_osb[usl - 2]], sem=("ua", "ub")[usl - 2])
            for grp in range(2):
                gb = 4 * (i % 2) + grp
                ub = 4 * (i % 2) + 2 + grp
                xres = [r_xn2T[4 * grp + k] for k in range(4)]
                em("tensor", lambda e, grp=grp, gsl=gsl, gb=gb: [
                    e.matmul(ph(gb), lhsT=gub[gsl][:, dc, :],
                             rhs=xn2T[:, dc, 512 * grp:512 * grp + 512],
                             start=(dc == 0), stop=(dc == DC - 1))
                    for dc in range(DC)][-1],
                   reads=xres + [r_gub[gsl]], writes=[r_pb[gb]])
                em("tensor", lambda e, grp=grp, usl=usl, ub=ub: [
                    e.matmul(ph(ub), lhsT=gub[usl][:, dc, :],
                             rhs=xn2T[:, dc, 512 * grp:512 * grp + 512],
                             start=(dc == 0), stop=(dc == DC - 1))
                    for dc in range(DC)][-1],
                   reads=xres + [r_gub[usl]], writes=[r_pb[ub]])
                em("scalar", lambda e, i=i, gb=gb: e.activation(
                    silub, ph(gb), AF.Silu, scale=gs_sb[:, i:i + 1]),
                   reads=[r_pb[gb], r_misc], writes=[r_silu],
                   alias=([r_pt[0], r_pt[1], r_wot[3]]
                          if (sp == 0 and i == 0 and grp == 0) else ()))
                if sp == 0 and i == 0 and grp == 0:
                    hal = ([r_ao[t] for t in range(NT)] +
                           [r_xb2[0], r_xb2[1], r_mk[0], r_mk[1]] +
                           list(r_ex))
                elif sp == 0 and i == 36 and grp == 0:
                    hal = [r_wot[1]]
                else:
                    hal = ()
                em("vector", lambda e, i=i, grp=grp, ub=ub:
                   e.scalar_tensor_tensor(
                       (h_a if i < 36 else h_b)[:, (i if i < 36 else i - 36),
                                                512 * grp:512 * grp + 512],
                       ph(ub), us_sb[:, i:i + 1], silub,
                       ALU.mult, ALU.mult),
                   reads=[r_pb[ub], r_silu, r_misc], writes=[r_h[i]],
                   alias=hal)
        em("sync", lambda e: e.dma_start(out=dsb_sb, in_=ddsb),
           writes=[r_dsb], alias=[r_xn2T[k] for k in range(8)], sem="ds")
        NCH = 4 * 11

        def dn_slot(ci):
            if ci % 2:
                return 1
            return 0 if (ci // 2) % 2 == 0 else 2

        def dn_dma(ci):
            dmg_, i4_ = divmod(ci, 11)
            sl = dn_slot(ci)
            if ci % 2 == 0:
                al = ([r_gub[0]] if ci == 0 else
                      ([r_xnst2] if ci == 2 else ()))
                em("gpsimd", lambda e, dmg_=dmg_, i4_=i4_, sl=sl: e.dma_start(
                    out=dnb[sl], in_=ddnt[dmg_, i4_], max_dma_last_dim=8192),
                   writes=[r_dnb[sl]], alias=al,
                   sem=("ua", "ub")[(ci // 2) % 2])
            else:
                fsl = (ci // 2) % 2
                em("sync", lambda e, dmg_=dmg_, i4_=i4_, fsl=fsl: e.dma_start(
                    out=dnf[fsl],
                    in_=ddnt[dmg_, i4_].rearrange("p k h -> p (k h)")),
                   writes=[r_dnf[fsl]],
                   alias=([r_xn2T[kk] for kk in range(8)] if ci == 1 else ()),
                   sem="ds")

        def dn_cast(ci):
            if ci % 2 == 0:
                return
            fsl = (ci // 2) % 2
            em("scalar", lambda e, fsl=fsl: e.copy(
                dnb[1].rearrange("p k h -> p (k h)"), dnf[fsl]),
               reads=[r_dnf[fsl]], writes=[r_dnb[1]],
               alias=([r_gub[1]] if ci == 1 else ()))

        def x1_reload(d):
            xsl = d % 2
            for tt in range(4):
                em("sync", lambda e, d=d, xsl=xsl, tt=tt, sp=sp: e.dma_start(
                    out=x1r[sp][xsl][:, tt, :],
                    in_=dx1[128 * (8 * sp + tt):128 * (8 * sp + tt) + 128,
                            512 * d:512 * d + 512]),
                   reads=[r_x1sp[8 * sp + tt]], writes=[r_x1r[xsl]],
                   alias=([r_gf[0], r_gf[1]] if tt == 0 else ()),
                   sem=("x2a", "x2b")[xsl])

        dn_dma(0)
        dn_dma(1)
        x1_reload(0)
        for ci in range(NCH):
            dmg, i4 = divmod(ci, 11)
            dsl = dn_slot(ci)
            if ci + 1 < NCH:
                dn_cast(ci + 1)
            if i4 == 5 and dmg + 1 < 4:
                x1_reload(dmg + 1)
            if sp == 0 and 4 <= ci < 36 and ci % 4 == 0:
                ln2_stats(1, (ci - 4) // 4)
            if i4 == 0:
                for tt in range(8):
                    em("tensor", lambda e, tt=tt, dsl=dsl:
                       e.matmul(ph(tt), lhsT=h_ap(0, tt),
                                rhs=dnb[dsl][:, 0, :],
                                start=True, stop=False),
                       reads=[r_h[0], r_dnb[dsl]],
                       writes=[r_pb[tt]])
                em("tensor", lambda e, dsl=dsl: [
                    e.matmul(ph(tt), lhsT=h_ap(k, tt),
                             rhs=dnb[dsl][:, k, :],
                             start=False, stop=False)
                    for k in range(1, 4) for tt in range(8)],
                   reads=[r_h[1], r_h[2], r_h[3], r_dnb[dsl]],
                   writes=[r_pb[tt] for tt in range(8)], n=24)
            else:
                em("tensor", lambda e, i4=i4, dsl=dsl: [
                    e.matmul(ph(tt), lhsT=h_ap(4 * i4 + k, tt),
                             rhs=dnb[dsl][:, k, :],
                             start=False, stop=(4 * i4 + k == IC - 1))
                    for k in range(4) for tt in range(8)],
                   reads=[r_h[4 * i4 + k] for k in range(4)] + [r_dnb[dsl]],
                   writes=[r_pb[tt] for tt in range(8)], n=32)
            if ci + 2 < NCH:
                dn_dma(ci + 2)
            if i4 != 10:
                continue
            for tt in range(8):
                t = 8 * sp + tt
                oslot = tt % 2
                em("vector", lambda e, t=t, tt=tt, dmg=dmg, oslot=oslot:
                   e.tensor_mul(osb[oslot], ph(tt),
                                dsb_sb[:, 512 * dmg:512 * dmg + 512]),
                   reads=[r_pb[tt], r_dsb], writes=[r_osb[oslot]],
                   alias=([r_gub[2 + oslot]]
                          if (sp == 0 and dmg == 0 and tt < 2) else ()))
                if tt < 4:
                    xsrc = x1r[sp][dmg % 2][:, tt, :]
                    xread = r_x1r[dmg % 2]
                else:
                    xsrc = x1_sb[:, t, 512 * dmg:512 * dmg + 512]
                    xread = r_x1[t]
                em("vector", lambda e, oslot=oslot, xsrc=xsrc:
                   e.tensor_add(osb[oslot], osb[oslot], xsrc),
                   reads=[xread], writes=[r_osb[oslot]])
                em("sync", lambda e, t=t, dmg=dmg, oslot=oslot: e.dma_start(
                    out=dout[128 * t:128 * t + 128,
                             512 * dmg:512 * dmg + 512],
                    in_=osb[oslot]),
                   reads=[r_osb[oslot]], sem=("sa", "sb")[oslot])

    if debug:
        dbg_dump("d_x1", x1_sb, [r_x1[t] for t in range(NT)])

    B.barrier_wait("sync", [("sa", B.P.cnt["sa"]), ("sb", B.P.cnt["sb"])])

    import contextlib
    stack = contextlib.ExitStack()
    sem_handles = {s: stack.enter_context(nc.semaphore(s)) for s in SEMS}
    blk = stack.enter_context(nc.Block())

    def runner(queue):
        def run(eng):
            for fn, waits, inc_sem, inc_n in queue:
                for s, v in waits:
                    eng.wait_ge(sem_handles[s], v)
                if fn is None:
                    continue
                inst = fn(eng)
                if isinstance(inst, list):
                    assert len(inst) == inc_n, (len(inst), inc_n)
                    for i_ in inst:
                        i_.then_inc(sem_handles[inc_sem], 1)
                elif inc_sem is not None:
                    inst.then_inc(sem_handles[inc_sem], inc_n)
        return run

    blk.tensor(runner(B.P.q["tensor"]))
    blk.vector(runner(B.P.q["vector"]))
    blk.scalar(runner(B.P.q["scalar"]))
    blk.sync(runner(B.P.q["sync"]))
    blk.gpsimd(runner(B.P.q["gpsimd"]))
    stack.close()
    return nc


_NC_CACHE = None


def kernel(L0_attn_x, L0_attn_wq, L0_attn_wk, L0_attn_wv, L0_attn_wo,
           L0_ffn_gate_w, L0_ffn_up_w, L0_ffn_down_w, L0_ffn_gate_s,
           L0_ffn_up_s, L0_ffn_down_s):
    global _NC_CACHE
    x = np.asarray(L0_attn_x, dtype=np.float32)
    wq = np.asarray(L0_attn_wq, np.float32)
    wk = np.asarray(L0_attn_wk, np.float32)
    wv = np.asarray(L0_attn_wv, np.float32)
    wo = np.asarray(L0_attn_wo, np.float32)
    gw = np.asarray(L0_ffn_gate_w, np.float32)
    uw = np.asarray(L0_ffn_up_w, np.float32)
    dw = np.asarray(L0_ffn_down_w, np.float32)
    gs = np.asarray(L0_ffn_gate_s, np.float32)
    us = np.asarray(L0_ffn_up_s, np.float32)
    ds = np.asarray(L0_ffn_down_s, np.float32)

    shared = {
        "wqt": _prep_lhsT(wq, 16),
        "wkt": _prep_lhsT(wk, 4),
        "wvt": _prep_wvt(wv),
        "wot": _prep_wot(wo),
        "gt": _prep_lhsT(gw, IC),
        "ut": _prep_lhsT(uw, IC),
        "dnt": _prep_dnt(dw),
        "gsc": np.ascontiguousarray(gs.reshape(IC, 128).T),
        "usc": np.ascontiguousarray(us.reshape(IC, 128).T),
        "dsb": np.ascontiguousarray(np.broadcast_to(ds, (128, D))),
        "ide": np.eye(128, dtype=np.float32),
        "cst": np.ascontiguousarray(np.broadcast_to(
            np.array([1.0 / D, EPS, -1.0, 0.0], np.float32), (128, 4))),
    }
    in_maps = []
    for core in range(8):
        s, h = core // 2, core % 2
        m = dict(shared)
        m["xo"] = np.ascontiguousarray(x[s, h * TOK:(h + 1) * TOK])
        m["xh"] = np.ascontiguousarray(
            x[s, 1792:2048] if h == 1 else x[s, 0:HALO])
        m["xs"] = np.ascontiguousarray(x[s, 0:SINK])
        m["mask"] = _make_mask(core)
        in_maps.append(m)

    if _NC_CACHE is None:
        _NC_CACHE = build_nc()
    res = run_bass_kernel_spmd(_NC_CACHE, in_maps, core_ids=list(range(8)))
    out = np.empty((4, 4096, D), dtype=np.float32)
    for core in range(8):
        s, h = core // 2, core % 2
        out[s, h * TOK:(h + 1) * TOK] = res.results[core]["out"]
    return out
